# version 10
# baseline (speedup 1.0000x reference)
import numpy as np
import ml_dtypes
import concourse.bass as bass
import concourse.mybir as mybir
from concourse.bass_utils import run_bass_kernel_spmd

F32 = mybir.dt.float32
BF16 = mybir.dt.bfloat16
AF = mybir.ActivationFunctionType
ALU = mybir.AluOpType
AX = mybir.AxisListType

P = 128
D = 2048
T = 2048
BC = 2
DCN = 16
NST = 64
RDT = 128
TT = 512
NTT = T // TT
NSUB = TT // P
EPS = 1e-5
NEXP = 8
LR = 16

ENGINES = ("sp", "act", "dve", "pool", "pe", "gp")
DMA_STEP = {"sp": 16, "gp": 16, "st": 16}


class Plan:
    def __init__(self):
        self.ops = {e: [] for e in ENGINES}
        self.count = {s: 0 for s in
                      ("sp", "gp", "st", "act", "dve", "pool", "pe",
                       "rdy", "rdyg")}
        self.lastwait = {e: {} for e in ENGINES}
        self.marks = []
        self.signals = {}

    def mark(self, name):
        self.marks.append((name, {e: len(self.ops[e]) for e in ENGINES}))

    DMA_OWNER = {"sp": "sp", "st": "sp", "gp": "gp"}
    RDY = {"sp": "rdy", "st": "rdy", "gp": "rdyg"}

    def _signal(self, qsem):
        rdy = self.RDY[qsem]
        covered = self.count[qsem]
        sigs = self.signals.setdefault(qsem, [])
        if sigs and sigs[-1][0] >= covered:
            return sigs[-1][1]
        self.count[rdy] += 1
        ev = (rdy, self.count[rdy])
        stream = "sp" if self.DMA_OWNER[qsem] == "sp" else "gp"
        self.ops[stream].append((("SIG", rdy), [(qsem, covered)], ev))
        sigs.append((covered, ev))
        return ev

    def emit(self, eng, fn, deps=(), inc=None, strict=False):
        waits = []
        for dep in deps:
            if dep is None:
                continue
            sem, val = dep
            if sem in DMA_STEP:
                owner = self.DMA_OWNER[sem]
                if (eng == owner) or (eng == "pool" and owner == "gp"):
                    val = self.count[sem]
                else:
                    ev2 = None
                    for cov, sev in self.signals.get(sem, []):
                        if cov >= val:
                            ev2 = sev
                            break
                    if ev2 is None:
                        ev2 = self._signal(sem)
                    sem, val = ev2
            if sem == eng and sem not in DMA_STEP and not strict:
                continue
            if self.lastwait[eng].get(sem, 0) >= val:
                continue
            self.lastwait[eng][sem] = val
            waits.append((sem, val))
        ev = None
        if inc is not None:
            self.count[inc] += DMA_STEP.get(inc, 1)
            ev = (inc, self.count[inc])
        self.ops[eng].append((fn, waits, ev))
        return ev

    def replay(self, eng, engobj, sems, nc=None):
        bounds = [(nm, idx[eng]) for nm, idx in self.marks]
        bi = 0
        import contextlib
        ctx = contextlib.nullcontext()
        opened = None
        for i, (fn, waits, ev) in enumerate(self.ops[eng]):
            while bi < len(bounds) and bounds[bi][1] == i:
                if opened is not None:
                    opened.__exit__(None, None, None)
                    opened = None
                if nc is not None:
                    opened = nc.named_scope(bounds[bi][0])
                    opened.__enter__()
                bi += 1
            for sem, val in waits:
                engobj.wait_ge(sems[sem], val)
            if isinstance(fn, tuple) and fn[0] == "SIG":
                engobj.sem_inc(sems[fn[1]], 1)
                continue
            inst = fn(engobj)
            if ev is not None:
                sem, _ = ev
                inst.then_inc(sems[sem], DMA_STEP.get(sem, 1))
        if opened is not None:
            opened.__exit__(None, None, None)


class Buf:
    def __init__(self, ap):
        self.ap = ap
        self.ready = None
        self.consumers = []

    def wdeps(self):
        return [self.ready] + self.consumers

    def wrote(self, ev):
        if ev is not None:
            self.ready = ev
        self.consumers = []

    def used(self, ev):
        if ev is not None:
            self.consumers.append(ev)


def build_program():
    nc = bass.Bass("TRN2", target_bir_lowering=False)

    x_in = nc.dram_tensor("x", [BC, T, D], F32, kind="ExternalInput").ap()
    wi = {}
    for nm, shp, dt in [
            ("normw", [D], F32), ("normb", [D], F32), ("xproj", [256, D], F32),
            ("dtw", [D, RDT], F32), ("dtb", [D], F32), ("alog", [D, NST], F32),
            ("dpar", [D], F32), ("opw", [D, D], F32), ("basew", [D, D], F32),
            ("baseb", [D], F32), ("routw", [NEXP, D], F32),
            ("routb", [NEXP], F32), ("loraA", [NEXP, D, LR], F32),
            ("loraB", [NEXP, LR, D], F32), ("idb", [P, P], BF16),
            ("idf", [P, P], F32), ("oneb", [1, P], BF16),
            ("selb", [LR, NEXP * P], BF16)]:
        wi[nm] = nc.dram_tensor(nm, shp, dt, kind="ExternalInput").ap()

    out_d = nc.dram_tensor("out", [BC, T, D], F32, kind="ExternalOutput").ap()

    xn_dram = nc.dram_tensor("xn_sc", [BC, D, T], BF16).ap()
    ln_dram = nc.dram_tensor("ln_sc", [BC, D, T], BF16).ap()
    z_dram = nc.dram_tensor("z_sc", [BC, D, T], BF16).ap()
    bc_dram = nc.dram_tensor("bc_sc", [BC, P, T], BF16).ap()
    x1_dram = nc.dram_tensor("x1_sc", [BC, T, D], F32).ap()
    opT_dram = nc.dram_tensor("opT_sc", [DCN, DCN, P, P], BF16).ap()
    baT_dram = nc.dram_tensor("baT_sc", [DCN, DCN, P, P], BF16).ap()
    g_dram = nc.dram_tensor("g_sc", [BC * NTT, 32, P], F32).ap()

    pl = Plan()
    ctxs = []

    def sb(name, shape, dtype):
        cm = nc.sbuf_tensor(name, shape, dtype)
        t = cm.__enter__()
        ctxs.append(cm)
        return Buf(t[:])

    def psb(name, shape, dtype):
        cm = nc.psum_tensor(name, shape, dtype)
        t = cm.__enter__()
        ctxs.append(cm)
        return Buf(t[:])

    xpT = sb("xpT", [P, 32 * P], BF16)
    dtT = sb("dtT", [P, DCN * P], BF16)
    A_sb = sb("A_sb", [P, DCN * NST], F32)
    laT = sb("laT", [P, DCN * NEXP * LR], BF16)
    lbT = sb("lbT", [P, D], BF16)
    selb = sb("selb_s", [LR, NEXP * P], BF16)
    rT = sb("rT", [P, DCN * NEXP], BF16)
    wbB = sb("wbB", [P, D], BF16)
    bbB = sb("bbB", [P, D], BF16)
    cst = sb("cst", [P, 4 * DCN], F32)
    rb_c = sb("rb_c", [NEXP, 1], F32)
    idb = sb("idb_s", [P, P], BF16)
    idf = sb("idf_s", [P, P], F32)
    oneb = sb("oneb_s", [1, P], BF16)
    dr_sb = sb("dr_sb", [P, BC * T], BF16)
    a0 = sb("a0", [P, T], F32)
    a1 = sb("a1", [P, T], F32)
    q0 = sb("q0", [P, T], BF16)
    q1 = sb("q1", [P, T], BF16)
    h0 = sb("h0", [P, T], BF16)
    h1 = sb("h1", [P, T], BF16)
    Bb0 = sb("Bb0", [P, T], BF16)
    Bb1 = sb("Bb1", [P, T], BF16)
    Cb0 = sb("Cb0", [P, 2 * T], BF16)
    Cb1 = sb("Cb1", [P, 2 * T], BF16)
    h0p = sb("h0p", [P, 2 * T], BF16)
    h1p = sb("h1p", [P, 2 * T], BF16)
    w0p = sb("w0p", [P, 2 * T], BF16)
    w1p = sb("w1p", [P, 2 * T], BF16)
    wb0 = sb("wb0", [P, T], BF16)
    wb1 = sb("wb1", [P, T], BF16)
    tree = [sb(f"lvl{i}", [P, T], BF16) for i in range(5)]
    spbuf = sb("spbuf", [P, T], F32)
    d0 = sb("d0", [P, T], F32)
    d1 = sb("d1", [P, T], F32)
    u0 = sb("u0", [P, T], BF16)
    u1 = sb("u1", [P, T], BF16)
    xns = sb("xns", [P, T], BF16)
    lns = sb("lns", [P, T], BF16)
    zbuf = sb("zbuf", [P, T], BF16)
    stat = sb("stat", [P, 16], F32)
    gsc = sb("gsc", [P, 112], F32)
    ps_a0 = psb("ps_a0", [P, 512], F32)
    ps_a1 = psb("ps_a1", [P, 512], F32)
    ps_t = psb("ps_t", [P, 2048], F32)
    ps_bt = psb("ps_bt", [P, 2048], BF16)

    def e_dma(q, out, in_, deps=(), inc="sp"):
        def fn(e):
            return e.dma_start(out=out, in_=in_)
        return pl.emit(q, fn, deps, inc)

    def e_tt(eng, out, in0, in1, op, deps=(), inc=None, strict=False):
        def fn(e):
            return e.tensor_tensor(out=out, in0=in0, in1=in1, op=op)
        return pl.emit(eng, fn, deps, inc, strict)

    def e_ts(eng, out, in0, scalar, op, deps=(), inc=None, strict=False):
        def fn(e):
            return e.tensor_single_scalar(out=out, in_=in0, scalar=scalar,
                                          op=op)
        return pl.emit(eng, fn, deps, inc, strict)

    def e_stt(eng, out, in0, scalar, in1, op0, op1, deps=(), inc=None,
              strict=False):
        def fn(e):
            return e.scalar_tensor_tensor(out=out, in0=in0, scalar=scalar,
                                          in1=in1, op0=op0, op1=op1)
        return pl.emit(eng, fn, deps, inc, strict)

    def e_act(out, in_, func, bias=0.0, scale=1.0, deps=(), inc=None,
              strict=False):
        def fn(e):
            return e.activation(out=out, in_=in_, func=func, bias=bias,
                                scale=scale)
        return pl.emit("act", fn, deps, inc, strict)

    def e_copy(eng, out, in_, deps=(), inc=None):
        def fn(e):
            return e.tensor_copy(out=out, in_=in_)
        return pl.emit(eng, fn, deps, inc)

    def e_red(out, in_, op, deps=(), inc=None, strict=False):
        def fn(e):
            return e.tensor_reduce(out=out, in_=in_, axis=AX.X, op=op)
        return pl.emit("dve", fn, deps, inc, strict)

    def e_mm(out, lhsT, rhs, start, stop, deps=(), inc=None):
        def fn(e):
            return e.matmul(out=out, lhsT=lhsT, rhs=rhs, start=start,
                            stop=stop)
        return pl.emit("pe", fn, deps, inc)

    def e_tr(out, in_, ident, deps=(), inc=None):
        def fn(e):
            return e.transpose(out=out, in_=in_, identity=ident)
        return pl.emit("pe", fn, deps, inc)

    def e_scan(out, data0, data1, deps=(), inc=None):
        def fn(e):
            return e.tensor_tensor_scan(out=out, data0=data0, data1=data1,
                                        initial=0.0, op0=ALU.mult,
                                        op1=ALU.add)
        return pl.emit("dve", fn, deps, inc)

    def e_recip(out, in_, deps=(), inc=None, strict=False):
        def fn(e):
            return e.reciprocal(out=out, in_=in_)
        return pl.emit("dve", fn, deps, inc, strict)

    for bufb, src in ((idb, wi["idb"]), (idf, wi["idf"]),
                      (oneb, wi["oneb"]), (selb, wi["selb"])):
        ev = e_dma("sp", bufb.ap, src)
        bufb.wrote(ev)

    xp_rows = (Bb0, Bb1)
    for c in range(2):
        ev = e_dma("gp", xp_rows[c].ap, wi["xproj"][c * P:(c + 1) * P, :],
                   inc="gp")
        xp_rows[c].wrote(ev)
    dt_rows = Cb0
    ev = e_dma("gp",
               dt_rows.ap[:, 0:DCN * RDT].rearrange("p (c r) -> p c r",
                                                    c=DCN),
               wi["dtw"].rearrange("(c p) r -> p c r", p=P), inc="gp")
    dt_rows.wrote(ev)
    al_buf = d0
    ev = e_dma("gp",
               al_buf.ap[:, 0:DCN * NST].rearrange("p (c n) -> p c n", c=DCN),
               wi["alog"].rearrange("(c p) n -> p c n", p=P), inc="gp")
    al_buf.wrote(ev)
    rw_rows = Cb1
    ev = e_dma("gp", rw_rows.ap[0:NEXP, 0:D], wi["routw"], inc="gp")
    rw_rows.wrote(ev)
    for e_ in range(NEXP):
        ev = e_dma("gp",
                   laT.ap.rearrange("p (c e r) -> p c e r", c=DCN,
                                    e=NEXP)[:, :, e_, :],
                   wi["loraA"][e_].rearrange("(c p) r -> p c r", p=P),
                   inc="gp")
    laT.wrote(ev)
    ev = e_dma("gp", lbT.ap, wi["loraB"].rearrange("e r d -> (e r) d"),
               inc="gp")
    lbT.wrote(ev)
    nwrow = q0
    ev = e_dma("gp", nwrow.ap[0:1, 0:D], wi["normw"][None, :], inc="gp")
    nwrow.wrote(ev)
    nbrow = q1
    ev = e_dma("gp", nbrow.ap[0:1, 0:D], wi["normb"][None, :], inc="gp")
    nbrow.wrote(ev)
    cols3 = d1
    ev = e_dma("gp", cols3.ap[0:DCN, 0:P],
               wi["dtb"].rearrange("(c p) -> c p", p=P), inc="gp")
    ev = e_dma("gp", cols3.ap[0:DCN, P:2 * P],
               wi["baseb"].rearrange("(c p) -> c p", p=P), inc="gp")
    ev = e_dma("gp", cols3.ap[0:DCN, 2 * P:3 * P],
               wi["dpar"].rearrange("(c p) -> c p", p=P), inc="gp")
    cols3.wrote(ev)
    ev = e_dma("gp", rb_c.ap, wi["routb"][:, None], inc="gp")
    rb_c.wrote(ev)

    ev = e_act(lbT.ap, lbT.ap, AF.Identity, scale=2.0, deps=lbT.wdeps(),
               inc="act")
    lbT.wrote(ev)
    ev = e_act(A_sb.ap, al_buf.ap[:, 0:DCN * NST], AF.Exp,
               deps=al_buf.wdeps() + A_sb.wdeps(), inc="act")
    al_buf.used(ev)
    ev = e_ts("dve", A_sb.ap, A_sb.ap, -1.0, ALU.mult, deps=[ev], inc="dve")
    A_sb.wrote(ev)

    def transpose_tile(src_ap, src_buf, dst_ap, dst_buf, ident=None,
                       pscols=P, psbuf=None, nrow=P):
        idbuf = ident or idb
        ps = psbuf or ps_bt
        id_ap = idbuf.ap if nrow == P else idbuf.ap[0:nrow, 0:nrow]
        evt = e_tr(ps.ap[:, 0:pscols] if nrow == P else
                   ps.ap[0:pscols, 0:nrow].ap if False else
                   ps.ap[:, 0:nrow][0:pscols, :] if False else
                   ps.ap[0:pscols, 0:nrow],
                   src_ap, id_ap,
                   deps=src_buf.wdeps() + ps.wdeps() + idbuf.wdeps(),
                   inc="pe")
        src_buf.used(evt)
        idbuf.used(evt)
        ps.wrote(evt)
        eve = e_copy("dve", dst_ap, ps.ap[0:pscols, 0:nrow],
                     deps=[evt] + dst_buf.wdeps(), inc="dve")
        ps.used(eve)
        dst_buf.wrote(eve)
        return eve

    for kc in range(DCN):
        for mc in range(2):
            transpose_tile(
                xp_rows[mc].ap[:, kc * P:(kc + 1) * P], xp_rows[mc],
                xpT.ap[:, (kc * 2 + mc) * P:(kc * 2 + mc + 1) * P], xpT)
    for kc in range(DCN):
        transpose_tile(dt_rows.ap[:, kc * RDT:(kc + 1) * RDT], dt_rows,
                       dtT.ap[:, kc * P:(kc + 1) * P], dtT)
    for kc in range(DCN):
        transpose_tile(rw_rows.ap[0:NEXP, kc * P:(kc + 1) * P], rw_rows,
                       rT.ap[:, kc * NEXP:(kc + 1) * NEXP], rT, nrow=NEXP)
    for i in range(3):
        transpose_tile(cols3.ap[0:DCN, i * P:(i + 1) * P], cols3,
                       cst.ap[:, i * DCN:(i + 1) * DCN], cst, ident=idf,
                       psbuf=ps_t, nrow=DCN)

    def bcast_row(src_ap, src_buf, dstb):
        last = None
        for c in range(4):
            evm = e_mm(ps_a0.ap, oneb.ap, src_ap[:, c * 512:(c + 1) * 512],
                       True, True,
                       deps=src_buf.wdeps() + oneb.wdeps() + ps_a0.wdeps(),
                       inc="pe")
            src_buf.used(evm)
            oneb.used(evm)
            ps_a0.wrote(evm)
            last = e_act(dstb.ap[:, c * 512:(c + 1) * 512], ps_a0.ap,
                         AF.Identity, deps=[evm] + dstb.wdeps(), inc="act")
            ps_a0.used(last)
        dstb.wrote(last)

    bcast_row(nwrow.ap[0:1, 0:D], nwrow, wbB)
    bcast_row(nbrow.ap[0:1, 0:D], nbrow, bbB)

    stage_w = h1

    def big_wT(w_ap, dst_dram, rb0, rb1):
        for mc in range(DCN):
            rb = (rb0, rb1)[mc % 2]
            ev_row = e_dma("gp", rb.ap, w_ap[mc * P:(mc + 1) * P, :],
                           deps=rb.wdeps(), inc="gp")
            rb.wrote(ev_row)
            for kc in range(DCN):
                transpose_tile(rb.ap[:, kc * P:(kc + 1) * P], rb,
                               stage_w.ap[:, kc * P:(kc + 1) * P], stage_w)
            ev_st = e_dma(
                "sp", dst_dram[mc].rearrange("kc p m -> p kc m"),
                stage_w.ap.rearrange("p (kc m) -> p kc m", kc=DCN),
                deps=stage_w.wdeps())
            stage_w.used(ev_st)

    big_wT(wi["opw"], opT_dram, u0, u1)
    big_wT(wi["basew"], baT_dram, u0, u1)

    pl.mark("phW_end")
    xsub = (a0, a1)
    ln0b = u0
    sqb = u1
    t1b = xns
    xnb = zbuf
    xn_stage = (h0, h1, q0, q1)
    ln_stage = (Bb0, Bb1, wb0, wb1)
    bc_stage = lns

    ev_bc_store = {}
    ev_xn_store = {}
    ev_ln_store = {}

    def layer_norm_tm(xbuf, outbuf, c0, scale_extra=None):
        st = stat.ap
        ev_mu = e_red(st[:, c0:c0 + 1], xbuf.ap, ALU.add,
                      deps=xbuf.wdeps() + stat.wdeps(), inc="dve")
        xbuf.used(ev_mu)
        ev_sq = e_act(sqb.ap, xbuf.ap, AF.Square,
                      deps=xbuf.wdeps() + sqb.wdeps(), inc="act")
        xbuf.used(ev_sq)
        sqb.wrote(ev_sq)
        ev_s2 = e_red(st[:, c0 + 1:c0 + 2], sqb.ap, ALU.add,
                      deps=sqb.wdeps(), inc="dve")
        sqb.used(ev_s2)
        e1_ = e_ts("dve", st[:, c0:c0 + 1], st[:, c0:c0 + 1], 1.0 / D,
                   ALU.mult, deps=[ev_mu], inc="dve", strict=True)
        e2_ = e_ts("dve", st[:, c0 + 1:c0 + 2], st[:, c0 + 1:c0 + 2],
                   1.0 / D, ALU.mult, deps=[ev_s2], inc="dve", strict=True)
        e3_ = e_tt("dve", st[:, c0 + 2:c0 + 3], st[:, c0:c0 + 1],
                   st[:, c0:c0 + 1], ALU.mult, deps=[e1_], inc="dve",
                   strict=True)
        e4_ = e_tt("dve", st[:, c0 + 1:c0 + 2], st[:, c0 + 1:c0 + 2],
                   st[:, c0 + 2:c0 + 3], ALU.subtract, deps=[e2_, e3_],
                   inc="dve", strict=True)
        ev_v = e_ts("dve", st[:, c0 + 1:c0 + 2], st[:, c0 + 1:c0 + 2], EPS,
                    ALU.add, deps=[e4_], inc="dve", strict=True)
        ev_l = e_act(st[:, c0 + 2:c0 + 3], st[:, c0 + 1:c0 + 2], AF.Ln,
                     deps=[ev_v], inc="act", strict=True)
        ev_r = e_act(st[:, c0 + 3:c0 + 4], st[:, c0 + 2:c0 + 3], AF.Exp,
                     scale=-0.5, deps=[ev_l], inc="act", strict=True)
        ev_b = e_stt("dve", st[:, c0 + 4:c0 + 5], st[:, c0:c0 + 1], -1.0,
                     st[:, c0 + 3:c0 + 4], ALU.mult, ALU.mult, deps=[ev_r],
                     inc="dve", strict=True)
        ev_o = e_act(outbuf.ap, xbuf.ap, AF.Identity, bias=st[:, c0 + 4:c0 + 5],
                     scale=st[:, c0 + 3:c0 + 4],
                     deps=[ev_b] + outbuf.wdeps(), inc="act")
        xbuf.used(ev_o)
        outbuf.wrote(ev_o)
        return ev_o

    def stage_transpose(srcb, stages, s):
        for qtr in range(4):
            evts = None
            for j in range(4):
                kc = qtr * 4 + j
                evts = e_tr(ps_bt.ap[:, j * P:(j + 1) * P],
                            srcb.ap[:, kc * P:(kc + 1) * P], idb.ap,
                            deps=srcb.wdeps() + ps_bt.wdeps() + idb.wdeps(),
                            inc="pe")
                srcb.used(evts)
                idb.used(evts)
            ps_bt.wrote(evts)
            stg = stages[qtr]
            dst = stg.ap.rearrange("p (kc t) -> p kc t",
                                   kc=4)[:, :, s * P:(s + 1) * P]
            eve = e_copy("dve", dst,
                         ps_bt.ap[:, 0:512].rearrange("p (j t) -> p j t",
                                                      j=4),
                         deps=[evts] + stg.wdeps(), inc="dve")
            ps_bt.used(eve)
            stg.wrote(eve)

    for b in range(BC):
        for tt in range(NTT):
            t0 = tt * TT
            for s in range(NSUB):
                si = s % 2
                xb = xsub[si]
                ev_x = e_dma("sp", xb.ap, x_in[b, t0 + s * P:t0 + s * P + P,
                                               :], deps=xb.wdeps())
                xb.wrote(ev_x)
                layer_norm_tm(xb, ln0b, 0)
                layer_norm_tm(ln0b, t1b, 5)
                ev_xn = e_tt("dve", xnb.ap, t1b.ap, wbB.ap, ALU.mult,
                             deps=t1b.wdeps() + wbB.wdeps() + xnb.wdeps(),
                             inc="dve")
                t1b.used(ev_xn)
                wbB.used(ev_xn)
                ev_xn = e_tt("dve", xnb.ap, xnb.ap, bbB.ap, ALU.add,
                             deps=[ev_xn] + bbB.wdeps(), inc="dve")
                bbB.used(ev_xn)
                xnb.wrote(ev_xn)
                stage_transpose(xnb, xn_stage, s)
                stage_transpose(ln0b, ln_stage, s)
            ev_mm = [None, None]
            for mc in range(2):
                psd = (ps_a0, ps_a1)[mc]
                for kc in range(DCN):
                    stg = xn_stage[kc // 4]
                    rhs = stg.ap[:, (kc % 4) * TT:(kc % 4 + 1) * TT]
                    lhsT = xpT.ap[:, (kc * 2 + mc) * P:(kc * 2 + mc + 1) * P]
                    e = e_mm(psd.ap, lhsT, rhs, kc == 0, kc == DCN - 1,
                             deps=stg.wdeps() + xpT.wdeps() + psd.wdeps(),
                             inc="pe")
                    stg.used(e)
                    xpT.used(e)
                    ev_mm[mc] = e
                psd.wrote(ev_mm[mc])
            ev_dr = e_act(dr_sb.ap[:, b * T + t0:b * T + t0 + TT], ps_a0.ap,
                          AF.Identity, deps=ps_a0.wdeps() + dr_sb.wdeps(),
                          inc="act")
            ps_a0.used(ev_dr)
            dr_sb.wrote(ev_dr)
            ev_bc = e_copy("dve", bc_stage.ap[:, 0:TT], ps_a1.ap,
                           deps=ps_a1.wdeps() + bc_stage.wdeps(), inc="dve")
            ps_a1.used(ev_bc)
            bc_stage.wrote(ev_bc)
            ev = e_dma("sp", bc_dram[b, :, t0:t0 + TT], bc_stage.ap[:, 0:TT],
                       deps=bc_stage.wdeps())
            bc_stage.used(ev)
            ev_bc_store[b] = ev
            for qtr in range(4):
                for stages, dram, evd in ((xn_stage, xn_dram, ev_xn_store),
                                          (ln_stage, ln_dram, ev_ln_store)):
                    stg = stages[qtr]
                    ev = e_dma(
                        "sp",
                        dram[b].rearrange("(kc p) t -> p kc t", p=P)
                        [:, qtr * 4:(qtr + 1) * 4, t0:t0 + TT],
                        stg.ap.rearrange("p (kc t) -> p kc t", kc=4),
                        deps=stg.wdeps())
                    stg.used(ev)
                    evd[b] = ev

    pl.mark("phA_end")
    abuf = (a0, a1)
    qbuf = (q0, q1)
    hpair = (h0p, h1p)
    wpair = (w0p, w1p)
    Bbb = (Bb0, Bb1)
    Cpair = (Cb0, Cb1)
    dbuf = (d0, d1)
    ubuf = (u0, u1)

    ev_z_store = {}
    free_tree = list(tree)
    ev_scanN = {}
    ev_wN = {}
    ev_qN = {}

    def emit_S_dc(b, dc):
        if True:
            di = dc % 2
            lhsT = dtT.ap[:, dc * P:(dc + 1) * P]
            ev_dt = None
            for j in range(4):
                rhs = dr_sb.ap[:, b * T + j * 512:b * T + (j + 1) * 512]
                ev_dt = e_mm(ps_t.ap[:, j * 512:(j + 1) * 512], lhsT, rhs,
                             True, True,
                             deps=dr_sb.wdeps() + dtT.wdeps() + ps_t.wdeps(),
                             inc="pe")
                dr_sb.used(ev_dt)
                dtT.used(ev_dt)
            ps_t.wrote(ev_dt)
            dcol = cst.ap[:, dc:dc + 1]
            dl = dbuf[di]
            ev1 = e_act(dl.ap, ps_t.ap, AF.Abs, bias=dcol,
                        deps=ps_t.wdeps() + dl.wdeps() + cst.wdeps(),
                        inc="act")
            ps_t.used(ev1)
            cst.used(ev1)
            ev2 = e_act(dl.ap, dl.ap, AF.Exp, scale=-1.0, deps=[ev1],
                        inc="act")
            ev3 = e_act(dl.ap, dl.ap, AF.Ln, bias=1.0, deps=[ev2], inc="act")
            ev4 = e_act(spbuf.ap, ps_t.ap, AF.Relu, bias=dcol,
                        deps=ps_t.wdeps() + spbuf.wdeps(), inc="act")
            ps_t.used(ev4)
            spbuf.wrote(ev4)
            ev_d = e_tt("dve", dl.ap, dl.ap, spbuf.ap, ALU.add,
                        deps=[ev3, ev4], inc="dve")
            spbuf.used(ev_d)
            dl.wrote(ev_d)
            ev_xl = e_dma("sp", xns.ap, xn_dram[b, dc * P:(dc + 1) * P, :],
                          deps=xns.wdeps() + [ev_xn_store.get(b)])
            xns.wrote(ev_xl)
            ub = ubuf[di]
            ev_u = e_tt("dve", ub.ap, dl.ap, xns.ap, ALU.mult,
                        deps=dl.wdeps() + xns.wdeps() + ub.wdeps(),
                        inc="dve")
            dl.used(ev_u)
            xns.used(ev_u)
            ub.wrote(ev_u)
            ev_ld = e_dma("sp", lns.ap, ln_dram[b, dc * P:(dc + 1) * P, :],
                          deps=lns.wdeps() + [ev_ln_store.get(b)])
            lns.wrote(ev_ld)

            stack = []

            def emit_w(k):
                hb = hpair[k % 2]
                Cb = Cpair[k % 2]
                wtgt = wpair[k % 2]
                ev_w = e_tt("dve", wtgt.ap, hb.ap, Cb.ap, ALU.mult,
                            deps=hb.wdeps() + Cb.wdeps() + wtgt.wdeps(),
                            inc="dve")
                hb.used(ev_w)
                Cb.used(ev_w)
                wtgt.wrote(ev_w)
                ev_wN[k] = ev_w

            def emit_pair(k):
                wp = wpair[k % 2]
                if k < NST // 2 - 1:
                    tb = free_tree.pop()
                    ev_sg = e_tt("dve", tb.ap, wp.ap[:, 0:T],
                                 wp.ap[:, T:2 * T], ALU.add,
                                 deps=wp.wdeps() + tb.wdeps(), inc="dve")
                    wp.used(ev_sg)
                    tb.wrote(ev_sg)
                    stack.append([0, tb])
                    while len(stack) >= 2 and stack[-1][0] == stack[-2][0]:
                        l2, b2 = stack.pop()
                        l1, b1 = stack.pop()
                        evm = e_tt("dve", b1.ap, b1.ap, b2.ap, ALU.add,
                                   deps=b1.wdeps() + b2.wdeps(), inc="dve")
                        b1.wrote(evm)
                        b2.used(evm)
                        free_tree.append(b2)
                        stack.append([l1 + 1, b1])
                else:
                    l1, b1 = stack[-1]
                    evm = e_tt("dve", b1.ap, b1.ap, wp.ap[:, 0:T], ALU.add,
                               deps=b1.wdeps() + wp.wdeps(), inc="dve")
                    b1.wrote(evm)
                    evm = e_tt("dve", b1.ap, b1.ap, wp.ap[:, T:2 * T],
                               ALU.add, deps=b1.wdeps() + wp.wdeps(),
                               inc="dve")
                    wp.used(evm)
                    b1.wrote(evm)

            for k in range(NST // 2):
                ns = (2 * k, 2 * k + 1)
                for n in ns:
                    Bb = Bbb[n % 2]
                    evB = e_dma("sp", Bb.ap,
                                bc_dram[b, n:n + 1, :].partition_broadcast(P),
                                deps=Bb.wdeps() + [ev_bc_store.get(b)])
                    Bb.wrote(evB)
                    Cb = Cpair[k % 2]
                    evC = e_dma("sp", Cb.ap[:, (n % 2) * T:(n % 2 + 1) * T],
                                bc_dram[b, NST + n:NST + n + 1, :]
                                .partition_broadcast(P),
                                deps=Cb.wdeps() + [ev_bc_store.get(b)])
                    Cb.wrote(evC)
                for n in ns:
                    ab = abuf[n % 2]
                    scl = A_sb.ap[:, dc * NST + n:dc * NST + n + 1]
                    ev_a = e_act(ab.ap, dl.ap, AF.Exp, scale=scl,
                                 deps=dl.wdeps() + ab.wdeps() +
                                 A_sb.wdeps(), inc="act")
                    dl.used(ev_a)
                    A_sb.used(ev_a)
                    ab.wrote(ev_a)
                for n in ns:
                    qb = qbuf[n % 2]
                    Bb = Bbb[n % 2]
                    ev_q = e_tt("dve", qb.ap, ub.ap, Bb.ap, ALU.mult,
                                deps=ub.wdeps() + Bb.wdeps() + qb.wdeps(),
                                inc="dve")
                    ub.used(ev_q)
                    Bb.used(ev_q)
                    qb.wrote(ev_q)
                if k >= 1:
                    emit_w(k - 1)
                if k >= 2:
                    emit_pair(k - 2)
                for n in ns:
                    ab = abuf[n % 2]
                    qb = qbuf[n % 2]
                    hb = hpair[k % 2]
                    ev_sc = e_scan(hb.ap[:, (n % 2) * T:(n % 2 + 1) * T],
                                   ab.ap, qb.ap,
                                   deps=ab.wdeps() + qb.wdeps() +
                                   hb.wdeps(), inc="dve")
                    ab.used(ev_sc)
                    qb.used(ev_sc)
                    hb.wrote(ev_sc)
                    ev_scanN[n] = ev_sc
            emit_w(NST // 2 - 1)
            emit_pair(NST // 2 - 2)
            emit_pair(NST // 2 - 1)
            while len(stack) > 1:
                l2, b2 = stack.pop()
                l1, b1 = stack.pop()
                evm = e_tt("dve", b1.ap, b1.ap, b2.ap, ALU.add,
                           deps=b1.wdeps() + b2.wdeps(), inc="dve")
                b1.wrote(evm)
                b2.used(evm)
                free_tree.append(b2)
                stack.append([l1 + 1, b1])
            _, ybf = stack.pop()
            ev_z = e_stt("dve", zbuf.ap, lns.ap, cst.ap[:, 2 * DCN + dc:
                                                        2 * DCN + dc + 1],
                         ybf.ap, ALU.mult, ALU.add,
                         deps=lns.wdeps() + ybf.wdeps() + zbuf.wdeps() +
                         cst.wdeps(), inc="dve")
            lns.used(ev_z)
            ybf.used(ev_z)
            cst.used(ev_z)
            zbuf.wrote(ev_z)
            free_tree.append(ybf)
            ev_zs = e_dma("sp", z_dram[b, dc * P:(dc + 1) * P, :], zbuf.ap,
                          deps=zbuf.wdeps())
            zbuf.used(ev_zs)
            ev_z_store[b] = ev_zs

    pl.mark("phS_end")
    ztb = (q0, q1, h0, h1)
    lnstage = (Bb0, Bb1, lns, tree[4])
    afb = (wb0, wb1, tree[0], tree[1])
    lwb = (tree[2], tree[3])
    x1b = spbuf
    xrl = (d0, d1)
    sqp = u0
    ln1b = u1
    midb = [Buf(xns.ap[0:LR, i * TT:(i + 1) * TT]) for i in range(2)]
    midgb = [Buf(xns.ap[0:LR, (2 + i) * TT:(3 + i) * TT]) for i in range(2)]
    mpack = Buf(zbuf.ap[:, 0:TT])
    gbb = [Buf(a0.ap[0:LR, i * TT:(i + 1) * TT]) for i in range(2)]
    lgb = Buf(a1.ap[0:NEXP, 0:TT])
    gTs = Buf(a1.ap[0:32, TT:TT + P])
    subs = {"xns": (xns, midb + midgb), "zbuf": (zbuf, [mpack]),
            "a0": (a0, gbb), "a1": (a1, [lgb, gTs])}

    def parent_sync(name):
        par, kids = subs[name]
        for k in kids:
            k.ready = par.ready
            k.consumers = list(par.consumers)

    for nm in subs:
        parent_sync(nm)

    def emit_P_tt(b, tt):
        if True:
            t0 = tt * TT
            tdi = b * NTT + tt
            for qtr in range(4):
                zt = ztb[qtr]
                ev = e_dma(
                    "sp", zt.ap.rearrange("p (kc t) -> p kc t", kc=4),
                    z_dram[b].rearrange("(kc p) t -> p kc t", p=P)
                    [:, qtr * 4:(qtr + 1) * 4, t0:t0 + TT],
                    deps=zt.wdeps() + [ev_z_store.get(b)])
                zt.wrote(ev)
            for mc in range(DCN):
                lw = lwb[mc % 2]
                ev_lw = e_dma("sp",
                              lw.ap.rearrange("p (kc m) -> p kc m", kc=DCN),
                              opT_dram[mc].rearrange("kc p m -> p kc m"),
                              deps=lw.wdeps())
                lw.wrote(ev_lw)
                psd = (ps_a0, ps_a1)[mc % 2]
                evm = None
                for kc in range(DCN):
                    zt = ztb[kc // 4]
                    rhs = zt.ap[:, (kc % 4) * TT:(kc % 4 + 1) * TT]
                    evm = e_mm(psd.ap, lw.ap[:, kc * P:(kc + 1) * P], rhs,
                               kc == 0, kc == DCN - 1,
                               deps=zt.wdeps() + lw.wdeps() + psd.wdeps(),
                               inc="pe")
                    zt.used(evm)
                    lw.used(evm)
                psd.wrote(evm)
                af = afb[mc // 4]
                ev = e_act(af.ap[:, (mc % 4) * TT:(mc % 4 + 1) * TT], psd.ap,
                           AF.Identity, deps=psd.wdeps() + af.wdeps(), inc="act")
                psd.used(ev)
                af.wrote(ev)
            for s in range(NSUB):
                si = s % 2
                xr = xrl[si]
                ev_x = e_dma("sp", xr.ap,
                             x_in[b, t0 + s * P:t0 + s * P + P, :],
                             deps=xr.wdeps())
                xr.wrote(ev_x)
                evt = None
                for mc in range(DCN):
                    af = afb[mc // 4]
                    src = af.ap[:, (mc % 4) * TT + s * P:
                                (mc % 4) * TT + (s + 1) * P]
                    evt = e_tr(ps_bt.ap[:, mc * P:(mc + 1) * P], src, idb.ap,
                               deps=af.wdeps() + ps_bt.wdeps() + idb.wdeps(),
                               inc="pe")
                    af.used(evt)
                    idb.used(evt)
                ps_bt.wrote(evt)
                ev_x1 = e_tt("dve", x1b.ap, xr.ap, ps_bt.ap, ALU.add,
                             deps=xr.wdeps() + ps_bt.wdeps() + x1b.wdeps(),
                             inc="dve")
                xr.used(ev_x1)
                ps_bt.used(ev_x1)
                x1b.wrote(ev_x1)
                ev_sp = e_dma("sp", x1_dram[b, t0 + s * P:t0 + s * P + P, :],
                              x1b.ap, deps=x1b.wdeps())
                x1b.used(ev_sp)
                layer_norm_tm(x1b, ln1b, 0)
                stage_transpose(ln1b, lnstage, s)
            evm = None
            for kc in range(DCN):
                stg = lnstage[kc // 4]
                rhs = stg.ap[:, (kc % 4) * TT:(kc % 4 + 1) * TT]
                evm = e_mm(ps_a0.ap[0:NEXP, :],
                           rT.ap[:, kc * NEXP:(kc + 1) * NEXP], rhs,
                           kc == 0, kc == DCN - 1,
                           deps=stg.wdeps() + rT.wdeps() + ps_a0.wdeps(),
                           inc="pe")
                stg.used(evm)
                rT.used(evm)
            ps_a0.wrote(evm)
            parent_sync("a1")
            ev_lg = e_act(lgb.ap, ps_a0.ap[0:NEXP, :], AF.Identity, bias=rb_c.ap,
                          deps=ps_a0.wdeps() + lgb.wdeps() + rb_c.wdeps(),
                          inc="act")
            ps_a0.used(ev_lg)
            rb_c.used(ev_lg)
            lgb.wrote(ev_lg)
            a1.wrote(ev_lg)
            for s in range(NSUB):
                evt = e_tr(ps_t.ap[0:P, s * NEXP:(s + 1) * NEXP],
                           lgb.ap[0:NEXP, s * P:(s + 1) * P],
                           idf.ap[0:NEXP, 0:NEXP],
                           deps=lgb.wdeps() + ps_t.wdeps() + idf.wdeps(),
                           inc="pe")
                lgb.used(evt)
                a1.used(evt)
                idf.used(evt)
                ps_t.wrote(evt)
                eve = e_copy("dve",
                             gsc.ap[:, 32 + s * NEXP:32 + (s + 1) * NEXP],
                             ps_t.ap[0:P, s * NEXP:(s + 1) * NEXP],
                             deps=[evt] + gsc.wdeps(), inc="dve")
                ps_t.used(eve)
                gsc.wrote(eve)
            for s in range(NSUB):
                lg = gsc.ap[:, 32 + s * NEXP:32 + (s + 1) * NEXP]
                m1 = gsc.ap[:, 64:65]
                nm1 = gsc.ap[:, 65 + s:66 + s]
                m2 = gsc.ap[:, 69:70]
                ddc = gsc.ap[:, 70 + s:71 + s]
                dnc = gsc.ap[:, 74:75]
                ivc = gsc.ap[:, 75:76]
                mask1 = gsc.ap[:, 80:88]
                lg2 = gsc.ap[:, 96:104]
                mask2 = gsc.ap[:, 104:112]
                ev1 = e_red(m1, lg, ALU.max, deps=gsc.wdeps() + stat.wdeps(),
                            inc="dve")
                evn = e_ts("dve", nm1, m1, -1.0, ALU.mult, deps=[ev1],
                           inc="dve", strict=True)
                ev3 = e_ts("dve", mask1, lg, m1, ALU.is_equal, deps=[ev1],
                           inc="dve", strict=True)
                ev4 = e_stt("dve", lg2, mask1, -1e30, lg, ALU.mult, ALU.add,
                            deps=[ev3], inc="dve", strict=True)
                ev5 = e_red(m2, lg2, ALU.max, deps=[ev4], inc="dve",
                            strict=True)
                ev6 = e_ts("dve", mask2, lg2, m2, ALU.is_equal, deps=[ev5],
                           inc="dve", strict=True)
                ev7 = e_act(lg, lg, AF.Exp, bias=nm1, deps=[ev5, evn],
                            inc="act", strict=True)
                ev8 = e_tt("dve", mask1, mask1, mask2, ALU.add,
                           deps=[ev6, ev7], inc="dve", strict=True)
                ev9 = e_tt("dve", ddc, m2, m1, ALU.subtract, deps=[ev5],
                           inc="dve", strict=True)
                ev10 = e_act(dnc, ddc, AF.Exp, deps=[ev9], inc="act",
                             strict=True)
                ev11 = e_ts("dve", dnc, dnc, 1.0, ALU.add, deps=[ev10],
                            inc="dve", strict=True)
                ev12 = e_recip(ivc, dnc, deps=[ev11], inc="dve", strict=True)
                ev13 = e_tt("dve", lg, lg, mask1, ALU.mult, deps=[ev7, ev8],
                            inc="dve", strict=True)
                gcols = gsc.ap[:, 0:32].rearrange(
                    "p (e s2) -> p s2 e", s2=4)[:, s, :]
                ev14 = e_ts("dve", gcols, lg,
                            ivc, ALU.mult, deps=[ev12, ev13], inc="dve",
                            strict=True)
                gsc.wrote(ev14)
            evt = e_tr(ps_t.ap[0:32, 0:P], gsc.ap[:, 0:32], idf.ap,
                       deps=gsc.wdeps() + ps_t.wdeps() + idf.wdeps(),
                       inc="pe")
            gsc.used(evt)
            idf.used(evt)
            ps_t.wrote(evt)
            parent_sync("a1")
            eve = e_copy("dve", gTs.ap, ps_t.ap[0:32, 0:P],
                         deps=[evt] + gTs.wdeps(), inc="dve")
            ps_t.used(eve)
            gTs.wrote(eve)
            a1.wrote(eve)
            ev_gst = e_dma("sp", g_dram[tdi], gTs.ap,
                           deps=gTs.wdeps())
            gTs.used(ev_gst)
            a1.used(ev_gst)
            parent_sync("xns")
            parent_sync("a0")
            parent_sync("zbuf")
            ev_pack = None
            for e_ in range(NEXP):
                ei = e_ % 2
                evm = None
                for kc in range(DCN):
                    stg = lnstage[kc // 4]
                    rhs = stg.ap[:, (kc % 4) * TT:(kc % 4 + 1) * TT]
                    lhsT = laT.ap[:, (kc * NEXP + e_) * LR:
                                  (kc * NEXP + e_ + 1) * LR]
                    evm = e_mm(ps_a0.ap[0:LR, :], lhsT, rhs, kc == 0,
                               kc == DCN - 1,
                               deps=stg.wdeps() + laT.wdeps() +
                               ps_a0.wdeps(), inc="pe")
                    stg.used(evm)
                    laT.used(evm)
                ps_a0.wrote(evm)
                md = midb[ei]
                ev_md = e_act(md.ap, ps_a0.ap[0:LR, :], AF.Identity,
                              deps=ps_a0.wdeps() + md.wdeps(), inc="act")
                ps_a0.used(ev_md)
                md.wrote(ev_md)
                xns.used(ev_md)
                gb = gbb[ei]
                gsrc = g_dram[tdi].rearrange(
                    "(e s2) j -> e (s2 j)", s2=4)[e_:e_ + 1, :]
                ev_gb = e_dma("sp", gb.ap,
                              gsrc.partition_broadcast(LR),
                              deps=gb.wdeps() + [ev_gst])
                gb.wrote(ev_gb)
                a0.used(ev_gb)
                mg = midgb[ei]
                ev_mg = e_tt("dve", mg.ap, md.ap, gb.ap, ALU.mult,
                             deps=md.wdeps() + gb.wdeps() + mg.wdeps(),
                             inc="dve")
                md.used(ev_mg)
                gb.used(ev_mg)
                mg.wrote(ev_mg)
                xns.used(ev_mg)
                ev_pack = e_mm(ps_a1.ap, selb.ap[:, e_ * P:(e_ + 1) * P],
                               mg.ap, e_ == 0, e_ == NEXP - 1,
                               deps=mg.wdeps() + selb.wdeps() +
                               ps_a1.wdeps(), inc="pe")
                mg.used(ev_pack)
                selb.used(ev_pack)
            ps_a1.wrote(ev_pack)
            ev_mp = e_act(mpack.ap, ps_a1.ap, AF.Identity,
                          deps=ps_a1.wdeps() + mpack.wdeps(), inc="act")
            ps_a1.used(ev_mp)
            mpack.wrote(ev_mp)
            zbuf.wrote(ev_mp)
            for mc in range(DCN):
                lw = lwb[mc % 2]
                ev_lw = e_dma("sp",
                              lw.ap.rearrange("p (kc m) -> p kc m", kc=DCN),
                              baT_dram[mc].rearrange("kc p m -> p kc m"),
                              deps=lw.wdeps())
                lw.wrote(ev_lw)
                psd = (ps_a0, ps_a1)[mc % 2]
                evm = e_mm(psd.ap, lbT.ap[:, mc * P:(mc + 1) * P], mpack.ap,
                           True, False,
                           deps=mpack.wdeps() + lbT.wdeps() + psd.wdeps(),
                           inc="pe")
                mpack.used(evm)
                zbuf.used(evm)
                lbT.used(evm)
                for kc in range(DCN):
                    stg = lnstage[kc // 4]
                    rhs = stg.ap[:, (kc % 4) * TT:(kc % 4 + 1) * TT]
                    evm = e_mm(psd.ap, lw.ap[:, kc * P:(kc + 1) * P], rhs,
                               False, kc == DCN - 1,
                               deps=stg.wdeps() + lw.wdeps(),
                               inc="pe")
                    stg.used(evm)
                    lw.used(evm)
                psd.wrote(evm)
                af = afb[mc // 4]
                ev = e_act(af.ap[:, (mc % 4) * TT:(mc % 4 + 1) * TT], psd.ap,
                           AF.Identity, bias=cst.ap[:, DCN + mc:DCN + mc + 1],
                           deps=psd.wdeps() + af.wdeps() + cst.wdeps(),
                           inc="act")
                psd.used(ev)
                cst.used(ev)
                af.wrote(ev)
            for s in range(NSUB):
                si = s % 2
                xr = xrl[si]
                ev_x1r = e_dma("sp", xr.ap,
                               x1_dram[b, t0 + s * P:t0 + s * P + P, :],
                               deps=xr.wdeps())
                xr.wrote(ev_x1r)
                evt = None
                for mc in range(DCN):
                    af = afb[mc // 4]
                    src = af.ap[:, (mc % 4) * TT + s * P:
                                (mc % 4) * TT + (s + 1) * P]
                    evt = e_tr(ps_bt.ap[:, mc * P:(mc + 1) * P], src, idb.ap,
                               deps=af.wdeps() + ps_bt.wdeps() + idb.wdeps(),
                               inc="pe")
                    af.used(evt)
                    idb.used(evt)
                ps_bt.wrote(evt)
                ev_o = e_tt("dve", xr.ap, xr.ap, ps_bt.ap, ALU.add,
                            deps=xr.wdeps() + ps_bt.wdeps(), inc="dve")
                ps_bt.used(ev_o)
                xr.wrote(ev_o)
                ev_os = e_dma("sp", out_d[b, t0 + s * P:t0 + s * P + P, :],
                              xr.ap, deps=xr.wdeps(), inc="st")
                xr.used(ev_os)

    for dc in range(DCN):
        emit_S_dc(0, dc)
    pl.mark("b0S_end")
    for dc in range(DCN):
        emit_S_dc(1, dc)
        if dc % 4 == 1:
            emit_P_tt(0, dc // 4)
    pl.mark("b1S_end")
    for tt in range(NTT):
        emit_P_tt(1, tt)
    pl.mark("b1P_end")

    n_out = pl.count["st"]

    def fin(e):
        return e.nop()
    pl.emit("sp", fin, deps=[("st", n_out)], inc=None)

    with (
        nc.semaphore("s_sp") as s_sp,
        nc.semaphore("s_gp") as s_gp,
        nc.semaphore("s_st") as s_st,
        nc.semaphore("s_act") as s_act,
        nc.semaphore("s_dve") as s_dve,
        nc.semaphore("s_pool") as s_pool,
        nc.semaphore("s_pe") as s_pe,
        nc.semaphore("s_rdy") as s_rdy,
        nc.semaphore("s_rdyg") as s_rdyg,
        nc.Block() as block,
    ):
        sems = dict(sp=s_sp, gp=s_gp, st=s_st, act=s_act, dve=s_dve,
                    pool=s_pool, pe=s_pe, rdy=s_rdy, rdyg=s_rdyg)

        @block.sync
        def _(eng):
            pl.replay("sp", eng, sems, nc)

        @block.scalar
        def _(eng):
            pl.replay("act", eng, sems, nc)

        @block.vector
        def _(eng):
            pl.replay("dve", eng, sems, nc)

        @block.gpsimd
        def _(eng):
            pl.replay("gp", eng, sems, nc)
            pl.replay("pool", eng, sems, nc)

        @block.tensor
        def _(eng):
            pl.replay("pe", eng, sems, nc)

    for cm in reversed(ctxs):
        cm.__exit__(None, None, None)

    return nc


_NC_CACHE = None


def _consts():
    idb = np.eye(P, dtype=ml_dtypes.bfloat16)
    idf = np.eye(P, dtype=np.float32)
    oneb = np.ones((1, P), dtype=ml_dtypes.bfloat16)
    selb = np.zeros((LR, NEXP * P), dtype=ml_dtypes.bfloat16)
    for e in range(NEXP):
        for k in range(LR):
            selb[k, e * P + LR * e + k] = 1.0
    return idb, idf, oneb, selb


def kernel(L0_attn_x, L0_attn_norm_w, L0_attn_norm_b, L0_attn_x_proj_w,
           L0_attn_dt_proj_w, L0_attn_dt_proj_b, L0_attn_A_log,
           L0_attn_D_param, L0_attn_out_proj_w, L0_ffn_base_w, L0_ffn_base_b,
           L0_ffn_router_w, L0_ffn_router_b, L0_ffn_lora_A, L0_ffn_lora_B):
    global _NC_CACHE
    if _NC_CACHE is None:
        _NC_CACHE = build_program()
    nc = _NC_CACHE

    idb, idf, oneb, selb = _consts()
    x = np.ascontiguousarray(L0_attn_x)
    common = dict(
        normw=L0_attn_norm_w, normb=L0_attn_norm_b, xproj=L0_attn_x_proj_w,
        dtw=L0_attn_dt_proj_w, dtb=L0_attn_dt_proj_b, alog=L0_attn_A_log,
        dpar=L0_attn_D_param, opw=L0_attn_out_proj_w, basew=L0_ffn_base_w,
        baseb=L0_ffn_base_b, routw=L0_ffn_router_w, routb=L0_ffn_router_b,
        loraA=L0_ffn_lora_A, loraB=L0_ffn_lora_B,
        idb=idb, idf=idf, oneb=oneb, selb=selb,
    )
    common = {k: np.ascontiguousarray(v) for k, v in common.items()}
    in_maps = []
    for c in range(8):
        m = dict(common)
        m["x"] = x[c * BC:(c + 1) * BC]
        in_maps.append(m)
    res = run_bass_kernel_spmd(nc, in_maps, core_ids=list(range(8)))
    return np.concatenate([r["out"] for r in res.results], axis=0)


if __name__ == "__main__":
    nc = build_program()
    n = sum(1 for _ in nc.all_instructions())
    print("built ok, instructions:", n)


# version 11
# speedup vs baseline: 1.0266x; 1.0266x over previous
import numpy as np
import ml_dtypes
import concourse.bass as bass
import concourse.mybir as mybir
from concourse.bass_utils import run_bass_kernel_spmd

F32 = mybir.dt.float32
BF16 = mybir.dt.bfloat16
AF = mybir.ActivationFunctionType
ALU = mybir.AluOpType
AX = mybir.AxisListType

P = 128
D = 2048
T = 2048
BC = 2
DCN = 16
NST = 64
RDT = 128
TT = 512
NTT = T // TT
NSUB = TT // P
EPS = 1e-5
NEXP = 8
LR = 16

ENGINES = ("sp", "act", "dve", "pool", "pe", "gp")
DMA_STEP = {"sp": 16, "gp": 16, "st": 16}


class Plan:
    def __init__(self):
        self.ops = {e: [] for e in ENGINES}
        self.count = {s: 0 for s in
                      ("sp", "gp", "st", "act", "dve", "pool", "pe",
                       "rdy", "rdyg")}
        self.lastwait = {e: {} for e in ENGINES}
        self.marks = []
        self.signals = {}

    def mark(self, name):
        self.marks.append((name, {e: len(self.ops[e]) for e in ENGINES}))

    DMA_OWNER = {"sp": "sp", "st": "sp", "gp": "gp"}
    RDY = {"sp": "rdy", "st": "rdy", "gp": "rdyg"}

    def _signal(self, qsem):
        rdy = self.RDY[qsem]
        covered = self.count[qsem]
        sigs = self.signals.setdefault(qsem, [])
        if sigs and sigs[-1][0] >= covered:
            return sigs[-1][1]
        self.count[rdy] += 1
        ev = (rdy, self.count[rdy])
        stream = "sp" if self.DMA_OWNER[qsem] == "sp" else "gp"
        self.ops[stream].append((("SIG", rdy), [(qsem, covered)], ev))
        sigs.append((covered, ev))
        return ev

    def emit(self, eng, fn, deps=(), inc=None, strict=False):
        waits = []
        for dep in deps:
            if dep is None:
                continue
            sem, val = dep
            if sem in DMA_STEP:
                owner = self.DMA_OWNER[sem]
                if (eng == owner) or (eng == "pool" and owner == "gp"):
                    val = self.count[sem]
                else:
                    ev2 = None
                    for cov, sev in self.signals.get(sem, []):
                        if cov >= val:
                            ev2 = sev
                            break
                    if ev2 is None:
                        ev2 = self._signal(sem)
                    sem, val = ev2
            if sem == eng and sem not in DMA_STEP and not strict:
                continue
            if self.lastwait[eng].get(sem, 0) >= val:
                continue
            self.lastwait[eng][sem] = val
            waits.append((sem, val))
        ev = None
        if inc is not None:
            self.count[inc] += DMA_STEP.get(inc, 1)
            ev = (inc, self.count[inc])
        self.ops[eng].append((fn, waits, ev))
        return ev

    def replay(self, eng, engobj, sems, nc=None):
        bounds = [(nm, idx[eng]) for nm, idx in self.marks]
        bi = 0
        import contextlib
        ctx = contextlib.nullcontext()
        opened = None
        for i, (fn, waits, ev) in enumerate(self.ops[eng]):
            while bi < len(bounds) and bounds[bi][1] == i:
                if opened is not None:
                    opened.__exit__(None, None, None)
                    opened = None
                if nc is not None:
                    opened = nc.named_scope(bounds[bi][0])
                    opened.__enter__()
                bi += 1
            for sem, val in waits:
                engobj.wait_ge(sems[sem], val)
            if isinstance(fn, tuple) and fn[0] == "SIG":
                engobj.sem_inc(sems[fn[1]], 1)
                continue
            inst = fn(engobj)
            if ev is not None:
                sem, _ = ev
                inst.then_inc(sems[sem], DMA_STEP.get(sem, 1))
        if opened is not None:
            opened.__exit__(None, None, None)


class Buf:
    def __init__(self, ap):
        self.ap = ap
        self.ready = None
        self.consumers = []

    def wdeps(self):
        return [self.ready] + self.consumers

    def wrote(self, ev):
        if ev is not None:
            self.ready = ev
        self.consumers = []

    def used(self, ev):
        if ev is not None:
            self.consumers.append(ev)


def build_program():
    nc = bass.Bass("TRN2", target_bir_lowering=False)

    x_in = nc.dram_tensor("x", [BC, T, D], F32, kind="ExternalInput").ap()
    wi = {}
    for nm, shp, dt in [
            ("normw", [D], F32), ("normb", [D], F32), ("xproj", [256, D], F32),
            ("dtw", [D, RDT], F32), ("dtb", [D], F32), ("alog", [D, NST], F32),
            ("dpar", [D], F32), ("opw", [D, D], F32), ("basew", [D, D], F32),
            ("baseb", [D], F32), ("routw", [NEXP, D], F32),
            ("routb", [NEXP], F32), ("loraA", [NEXP, D, LR], F32),
            ("loraB", [NEXP, LR, D], F32), ("idb", [P, P], BF16),
            ("idf", [P, P], F32), ("oneb", [1, P], BF16),
            ("selb", [LR, NEXP * P], BF16)]:
        wi[nm] = nc.dram_tensor(nm, shp, dt, kind="ExternalInput").ap()

    out_d = nc.dram_tensor("out", [BC, T, D], F32, kind="ExternalOutput").ap()

    xn_dram = nc.dram_tensor("xn_sc", [BC, D, T], BF16).ap()
    ln_dram = nc.dram_tensor("ln_sc", [BC, D, T], BF16).ap()
    z_dram = nc.dram_tensor("z_sc", [BC, D, T], BF16).ap()
    bc_dram = nc.dram_tensor("bc_sc", [BC, P, T], BF16).ap()
    x1_dram = nc.dram_tensor("x1_sc", [BC, T, D], F32).ap()
    opT_dram = nc.dram_tensor("opT_sc", [DCN, DCN, P, P], BF16).ap()
    baT_dram = nc.dram_tensor("baT_sc", [DCN, DCN, P, P], BF16).ap()
    g_dram = nc.dram_tensor("g_sc", [BC * NTT, 32, P], BF16).ap()

    pl = Plan()
    ctxs = []

    def sb(name, shape, dtype):
        cm = nc.sbuf_tensor(name, shape, dtype)
        t = cm.__enter__()
        ctxs.append(cm)
        return Buf(t[:])

    def psb(name, shape, dtype):
        cm = nc.psum_tensor(name, shape, dtype)
        t = cm.__enter__()
        ctxs.append(cm)
        return Buf(t[:])

    xpT = sb("xpT", [P, 32 * P], BF16)
    dtT = sb("dtT", [P, DCN * P], BF16)
    A_sb = sb("A_sb", [P, DCN * NST], F32)
    laT = sb("laT", [P, DCN * NEXP * LR], BF16)
    lbT = sb("lbT", [P, D], BF16)
    selb = sb("selb_s", [LR, NEXP * P], BF16)
    rT = sb("rT", [P, DCN * NEXP], BF16)
    wbB = sb("wbB", [P, D], BF16)
    bbB = sb("bbB", [P, D], BF16)
    cst = sb("cst", [P, 4 * DCN], F32)
    rb_c = sb("rb_c", [NEXP, 1], F32)
    idb = sb("idb_s", [P, P], BF16)
    idf = sb("idf_s", [P, 32], F32)
    oneb = sb("oneb_s", [1, P], BF16)
    dr_sb = sb("dr_sb", [P, BC * T], BF16)
    a0 = sb("a0", [P, T], F32)
    a1 = sb("a1", [P, T], F32)
    q0 = sb("q0", [P, T], BF16)
    q1 = sb("q1", [P, T], BF16)
    Bb0 = sb("Bb0", [P, T], BF16)
    Bb1 = sb("Bb1", [P, T], BF16)
    Bb2 = sb("Bb2", [P, T], BF16)
    Bb3 = sb("Bb3", [P, T], BF16)
    Cb0 = sb("Cb0", [P, 2 * T], BF16)
    Cb1 = sb("Cb1", [P, 2 * T], BF16)
    h0p = sb("h0p", [P, 2 * T], BF16)
    h1p = sb("h1p", [P, 2 * T], BF16)
    w0p = sb("w0p", [P, 2 * T], BF16)
    w1p = sb("w1p", [P, 2 * T], BF16)
    wb0 = sb("wb0", [P, T], BF16)
    wb1 = sb("wb1", [P, T], BF16)
    tree = [sb(f"lvl{i}", [P, T], BF16) for i in range(5)]
    spbuf = sb("spbuf", [P, T], F32)
    d0 = sb("d0", [P, T], F32)
    d1 = sb("d1", [P, T], F32)
    u0 = sb("u0", [P, T], BF16)
    u1 = sb("u1", [P, T], BF16)
    xns = sb("xns", [P, T], BF16)
    xns1 = sb("xns1", [P, T], BF16)
    lns = sb("lns", [P, T], BF16)
    zbuf = sb("zbuf", [P, T], BF16)
    stat = sb("stat", [P, 16], F32)
    gsc = sb("gsc", [P, 112], F32)
    ps_a0 = psb("ps_a0", [P, 512], F32)
    ps_a1 = psb("ps_a1", [P, 512], F32)
    ps_t = psb("ps_t", [P, 2048], F32)
    ps_bt = psb("ps_bt", [P, 2048], BF16)

    def e_dma(q, out, in_, deps=(), inc="sp"):
        def fn(e):
            return e.dma_start(out=out, in_=in_)
        return pl.emit(q, fn, deps, inc)

    def e_tt(eng, out, in0, in1, op, deps=(), inc=None, strict=False):
        def fn(e):
            return e.tensor_tensor(out=out, in0=in0, in1=in1, op=op)
        return pl.emit(eng, fn, deps, inc, strict)

    def e_ts(eng, out, in0, scalar, op, deps=(), inc=None, strict=False):
        def fn(e):
            return e.tensor_single_scalar(out=out, in_=in0, scalar=scalar,
                                          op=op)
        return pl.emit(eng, fn, deps, inc, strict)

    def e_stt(eng, out, in0, scalar, in1, op0, op1, deps=(), inc=None,
              strict=False):
        def fn(e):
            return e.scalar_tensor_tensor(out=out, in0=in0, scalar=scalar,
                                          in1=in1, op0=op0, op1=op1)
        return pl.emit(eng, fn, deps, inc, strict)

    def e_act(out, in_, func, bias=0.0, scale=1.0, deps=(), inc=None,
              strict=False):
        def fn(e):
            return e.activation(out=out, in_=in_, func=func, bias=bias,
                                scale=scale)
        return pl.emit("act", fn, deps, inc, strict)

    def e_copy(eng, out, in_, deps=(), inc=None):
        def fn(e):
            return e.tensor_copy(out=out, in_=in_)
        return pl.emit(eng, fn, deps, inc)

    def e_red(out, in_, op, deps=(), inc=None, strict=False):
        def fn(e):
            return e.tensor_reduce(out=out, in_=in_, axis=AX.X, op=op)
        return pl.emit("dve", fn, deps, inc, strict)

    def e_mm(out, lhsT, rhs, start, stop, deps=(), inc=None):
        def fn(e):
            return e.matmul(out=out, lhsT=lhsT, rhs=rhs, start=start,
                            stop=stop)
        return pl.emit("pe", fn, deps, inc)

    def e_tr(out, in_, ident, deps=(), inc=None):
        def fn(e):
            return e.transpose(out=out, in_=in_, identity=ident)
        return pl.emit("pe", fn, deps, inc)

    def e_scan(out, data0, data1, deps=(), inc=None):
        def fn(e):
            return e.tensor_tensor_scan(out=out, data0=data0, data1=data1,
                                        initial=0.0, op0=ALU.mult,
                                        op1=ALU.add)
        return pl.emit("dve", fn, deps, inc)

    def e_recip(out, in_, deps=(), inc=None, strict=False):
        def fn(e):
            return e.reciprocal(out=out, in_=in_)
        return pl.emit("dve", fn, deps, inc, strict)

    for bufb, src in ((idb, wi["idb"]), (idf, wi["idf"][:, 0:32]),
                      (oneb, wi["oneb"]), (selb, wi["selb"])):
        ev = e_dma("sp", bufb.ap, src)
        bufb.wrote(ev)

    xp_rows = (Bb0, Bb1)
    for c in range(2):
        ev = e_dma("gp", xp_rows[c].ap, wi["xproj"][c * P:(c + 1) * P, :],
                   inc="gp")
        xp_rows[c].wrote(ev)
    dt_rows = Cb0
    ev = e_dma("gp",
               dt_rows.ap[:, 0:DCN * RDT].rearrange("p (c r) -> p c r",
                                                    c=DCN),
               wi["dtw"].rearrange("(c p) r -> p c r", p=P), inc="gp")
    dt_rows.wrote(ev)
    al_buf = d0
    ev = e_dma("gp",
               al_buf.ap[:, 0:DCN * NST].rearrange("p (c n) -> p c n", c=DCN),
               wi["alog"].rearrange("(c p) n -> p c n", p=P), inc="gp")
    al_buf.wrote(ev)
    rw_rows = Cb1
    ev = e_dma("gp", rw_rows.ap[0:NEXP, 0:D], wi["routw"], inc="gp")
    rw_rows.wrote(ev)
    for e_ in range(NEXP):
        ev = e_dma("gp",
                   laT.ap.rearrange("p (c e r) -> p c e r", c=DCN,
                                    e=NEXP)[:, :, e_, :],
                   wi["loraA"][e_].rearrange("(c p) r -> p c r", p=P),
                   inc="gp")
    laT.wrote(ev)
    ev = e_dma("gp", lbT.ap, wi["loraB"].rearrange("e r d -> (e r) d"),
               inc="gp")
    lbT.wrote(ev)
    nwrow = q0
    ev = e_dma("gp", nwrow.ap[0:1, 0:D], wi["normw"][None, :], inc="gp")
    nwrow.wrote(ev)
    nbrow = q1
    ev = e_dma("gp", nbrow.ap[0:1, 0:D], wi["normb"][None, :], inc="gp")
    nbrow.wrote(ev)
    cols3 = d1
    ev = e_dma("gp", cols3.ap[0:DCN, 0:P],
               wi["dtb"].rearrange("(c p) -> c p", p=P), inc="gp")
    ev = e_dma("gp", cols3.ap[0:DCN, P:2 * P],
               wi["baseb"].rearrange("(c p) -> c p", p=P), inc="gp")
    ev = e_dma("gp", cols3.ap[0:DCN, 2 * P:3 * P],
               wi["dpar"].rearrange("(c p) -> c p", p=P), inc="gp")
    cols3.wrote(ev)
    ev = e_dma("gp", rb_c.ap, wi["routb"][:, None], inc="gp")
    rb_c.wrote(ev)

    ev = e_act(lbT.ap, lbT.ap, AF.Identity, scale=2.0, deps=lbT.wdeps(),
               inc="act")
    lbT.wrote(ev)
    ev = e_act(A_sb.ap, al_buf.ap[:, 0:DCN * NST], AF.Exp,
               deps=al_buf.wdeps() + A_sb.wdeps(), inc="act")
    al_buf.used(ev)
    ev = e_ts("dve", A_sb.ap, A_sb.ap, -1.0, ALU.mult, deps=[ev], inc="dve")
    A_sb.wrote(ev)

    def transpose_tile(src_ap, src_buf, dst_ap, dst_buf, ident=None,
                       pscols=P, psbuf=None, nrow=P):
        idbuf = ident or idb
        ps = psbuf or ps_bt
        id_ap = idbuf.ap if nrow == P else idbuf.ap[0:nrow, 0:nrow]
        evt = e_tr(ps.ap[:, 0:pscols] if nrow == P else
                   ps.ap[0:pscols, 0:nrow].ap if False else
                   ps.ap[:, 0:nrow][0:pscols, :] if False else
                   ps.ap[0:pscols, 0:nrow],
                   src_ap, id_ap,
                   deps=src_buf.wdeps() + ps.wdeps() + idbuf.wdeps(),
                   inc="pe")
        src_buf.used(evt)
        idbuf.used(evt)
        ps.wrote(evt)
        eve = e_copy("dve", dst_ap, ps.ap[0:pscols, 0:nrow],
                     deps=[evt] + dst_buf.wdeps(), inc="dve")
        ps.used(eve)
        dst_buf.wrote(eve)
        return eve

    for kc in range(DCN):
        for mc in range(2):
            transpose_tile(
                xp_rows[mc].ap[:, kc * P:(kc + 1) * P], xp_rows[mc],
                xpT.ap[:, (kc * 2 + mc) * P:(kc * 2 + mc + 1) * P], xpT)
    for kc in range(DCN):
        transpose_tile(dt_rows.ap[:, kc * RDT:(kc + 1) * RDT], dt_rows,
                       dtT.ap[:, kc * P:(kc + 1) * P], dtT)
    for kc in range(DCN):
        transpose_tile(rw_rows.ap[0:NEXP, kc * P:(kc + 1) * P], rw_rows,
                       rT.ap[:, kc * NEXP:(kc + 1) * NEXP], rT, nrow=NEXP)
    for i in range(3):
        transpose_tile(cols3.ap[0:DCN, i * P:(i + 1) * P], cols3,
                       cst.ap[:, i * DCN:(i + 1) * DCN], cst, ident=idf,
                       psbuf=ps_t, nrow=DCN)

    def bcast_row(src_ap, src_buf, dstb):
        last = None
        for c in range(4):
            evm = e_mm(ps_a0.ap, oneb.ap, src_ap[:, c * 512:(c + 1) * 512],
                       True, True,
                       deps=src_buf.wdeps() + oneb.wdeps() + ps_a0.wdeps(),
                       inc="pe")
            src_buf.used(evm)
            oneb.used(evm)
            ps_a0.wrote(evm)
            last = e_act(dstb.ap[:, c * 512:(c + 1) * 512], ps_a0.ap,
                         AF.Identity, deps=[evm] + dstb.wdeps(), inc="act")
            ps_a0.used(last)
        dstb.wrote(last)

    bcast_row(nwrow.ap[0:1, 0:D], nwrow, wbB)
    bcast_row(nbrow.ap[0:1, 0:D], nbrow, bbB)

    stage_w = wb0

    def big_wT(w_ap, dst_dram, rb0, rb1):
        for mc in range(DCN):
            rb = (rb0, rb1)[mc % 2]
            ev_row = e_dma("gp", rb.ap, w_ap[mc * P:(mc + 1) * P, :],
                           deps=rb.wdeps(), inc="gp")
            rb.wrote(ev_row)
            for kc in range(DCN):
                transpose_tile(rb.ap[:, kc * P:(kc + 1) * P], rb,
                               stage_w.ap[:, kc * P:(kc + 1) * P], stage_w)
            ev_st = e_dma(
                "sp", dst_dram[mc].rearrange("kc p m -> p kc m"),
                stage_w.ap.rearrange("p (kc m) -> p kc m", kc=DCN),
                deps=stage_w.wdeps())
            stage_w.used(ev_st)

    big_wT(wi["opw"], opT_dram, u0, u1)
    big_wT(wi["basew"], baT_dram, u0, u1)

    pl.mark("phW_end")
    xsub = (a0, a1)
    ln0b = u0
    sqb = u1
    t1b = xns
    xnb = zbuf
    xn_stage = (Bb2, Bb3, q0, q1)
    ln_stage = (Bb0, Bb1, wb0, wb1)
    bc_stage = lns

    ev_bc_store = {}
    ev_xn_store = {}
    ev_ln_store = {}

    def layer_norm_tm(xbuf, outbuf, c0, scale_extra=None):
        st = stat.ap
        ev_mu = e_red(st[:, c0:c0 + 1], xbuf.ap, ALU.add,
                      deps=xbuf.wdeps() + stat.wdeps(), inc="dve")
        xbuf.used(ev_mu)
        ev_sq = e_act(sqb.ap, xbuf.ap, AF.Square,
                      deps=xbuf.wdeps() + sqb.wdeps(), inc="act")
        xbuf.used(ev_sq)
        sqb.wrote(ev_sq)
        ev_s2 = e_red(st[:, c0 + 1:c0 + 2], sqb.ap, ALU.add,
                      deps=sqb.wdeps(), inc="dve")
        sqb.used(ev_s2)
        e1_ = e_ts("dve", st[:, c0:c0 + 1], st[:, c0:c0 + 1], 1.0 / D,
                   ALU.mult, deps=[ev_mu], inc="dve", strict=True)
        e2_ = e_ts("dve", st[:, c0 + 1:c0 + 2], st[:, c0 + 1:c0 + 2],
                   1.0 / D, ALU.mult, deps=[ev_s2], inc="dve", strict=True)
        e3_ = e_tt("dve", st[:, c0 + 2:c0 + 3], st[:, c0:c0 + 1],
                   st[:, c0:c0 + 1], ALU.mult, deps=[e1_], inc="dve",
                   strict=True)
        e4_ = e_tt("dve", st[:, c0 + 1:c0 + 2], st[:, c0 + 1:c0 + 2],
                   st[:, c0 + 2:c0 + 3], ALU.subtract, deps=[e2_, e3_],
                   inc="dve", strict=True)
        ev_v = e_ts("dve", st[:, c0 + 1:c0 + 2], st[:, c0 + 1:c0 + 2], EPS,
                    ALU.add, deps=[e4_], inc="dve", strict=True)
        ev_l = e_act(st[:, c0 + 2:c0 + 3], st[:, c0 + 1:c0 + 2], AF.Ln,
                     deps=[ev_v], inc="act", strict=True)
        ev_r = e_act(st[:, c0 + 3:c0 + 4], st[:, c0 + 2:c0 + 3], AF.Exp,
                     scale=-0.5, deps=[ev_l], inc="act", strict=True)
        ev_b = e_stt("dve", st[:, c0 + 4:c0 + 5], st[:, c0:c0 + 1], -1.0,
                     st[:, c0 + 3:c0 + 4], ALU.mult, ALU.mult, deps=[ev_r],
                     inc="dve", strict=True)
        ev_o = e_act(outbuf.ap, xbuf.ap, AF.Identity, bias=st[:, c0 + 4:c0 + 5],
                     scale=st[:, c0 + 3:c0 + 4],
                     deps=[ev_b] + outbuf.wdeps(), inc="act")
        xbuf.used(ev_o)
        outbuf.wrote(ev_o)
        return ev_o

    def stage_transpose(srcb, stages, s):
        for qtr in range(4):
            evts = None
            for j in range(4):
                kc = qtr * 4 + j
                evts = e_tr(ps_bt.ap[:, j * P:(j + 1) * P],
                            srcb.ap[:, kc * P:(kc + 1) * P], idb.ap,
                            deps=srcb.wdeps() + ps_bt.wdeps() + idb.wdeps(),
                            inc="pe")
                srcb.used(evts)
                idb.used(evts)
            ps_bt.wrote(evts)
            stg = stages[qtr]
            dst = stg.ap.rearrange("p (kc t) -> p kc t",
                                   kc=4)[:, :, s * P:(s + 1) * P]
            eve = e_copy("dve", dst,
                         ps_bt.ap[:, 0:512].rearrange("p (j t) -> p j t",
                                                      j=4),
                         deps=[evts] + stg.wdeps(), inc="dve")
            ps_bt.used(eve)
            stg.wrote(eve)

    for b in range(BC):
        for tt in range(NTT):
            t0 = tt * TT
            for s in range(NSUB):
                si = s % 2
                xb = xsub[si]
                ev_x = e_dma("sp", xb.ap, x_in[b, t0 + s * P:t0 + s * P + P,
                                               :], deps=xb.wdeps())
                xb.wrote(ev_x)
                layer_norm_tm(xb, ln0b, 0)
                layer_norm_tm(ln0b, t1b, 5)
                ev_xn = e_tt("dve", xnb.ap, t1b.ap, wbB.ap, ALU.mult,
                             deps=t1b.wdeps() + wbB.wdeps() + xnb.wdeps(),
                             inc="dve")
                t1b.used(ev_xn)
                wbB.used(ev_xn)
                ev_xn = e_tt("dve", xnb.ap, xnb.ap, bbB.ap, ALU.add,
                             deps=[ev_xn] + bbB.wdeps(), inc="dve")
                bbB.used(ev_xn)
                xnb.wrote(ev_xn)
                stage_transpose(xnb, xn_stage, s)
                stage_transpose(ln0b, ln_stage, s)
            ev_mm = [None, None]
            for mc in range(2):
                psd = (ps_a0, ps_a1)[mc]
                for kc in range(DCN):
                    stg = xn_stage[kc // 4]
                    rhs = stg.ap[:, (kc % 4) * TT:(kc % 4 + 1) * TT]
                    lhsT = xpT.ap[:, (kc * 2 + mc) * P:(kc * 2 + mc + 1) * P]
                    e = e_mm(psd.ap, lhsT, rhs, kc == 0, kc == DCN - 1,
                             deps=stg.wdeps() + xpT.wdeps() + psd.wdeps(),
                             inc="pe")
                    stg.used(e)
                    xpT.used(e)
                    ev_mm[mc] = e
                psd.wrote(ev_mm[mc])
            ev_dr = e_act(dr_sb.ap[:, b * T + t0:b * T + t0 + TT], ps_a0.ap,
                          AF.Identity, deps=ps_a0.wdeps() + dr_sb.wdeps(),
                          inc="act")
            ps_a0.used(ev_dr)
            dr_sb.wrote(ev_dr)
            ev_bc = e_copy("dve", bc_stage.ap[:, 0:TT], ps_a1.ap,
                           deps=ps_a1.wdeps() + bc_stage.wdeps(), inc="dve")
            ps_a1.used(ev_bc)
            bc_stage.wrote(ev_bc)
            ev = e_dma("sp", bc_dram[b, :, t0:t0 + TT], bc_stage.ap[:, 0:TT],
                       deps=bc_stage.wdeps())
            bc_stage.used(ev)
            ev_bc_store[b] = ev
            for qtr in range(4):
                for stages, dram, evd in ((xn_stage, xn_dram, ev_xn_store),
                                          (ln_stage, ln_dram, ev_ln_store)):
                    stg = stages[qtr]
                    ev = e_dma(
                        "sp",
                        dram[b].rearrange("(kc p) t -> p kc t", p=P)
                        [:, qtr * 4:(qtr + 1) * 4, t0:t0 + TT],
                        stg.ap.rearrange("p (kc t) -> p kc t", kc=4),
                        deps=stg.wdeps())
                    stg.used(ev)
                    evd[b] = ev

    pl.mark("phA_end")
    abuf = (a0, a1)
    qbuf = (q0, q1)
    hpair = (h0p, h1p)
    wpair = (w0p, w1p)
    Bbb = (Bb0, Bb1, Bb2, Bb3)
    Cpair = (Cb0, Cb1)
    dbuf = (d0, d1)
    ubuf = (u0, u1)
    xnsb = (xns, xns1)
    lnsb = (lns, lns)

    ev_z_store = {}
    free_tree = list(tree)
    ev_scanN = {}
    ev_wN = {}
    ev_qN = {}

    def emit_S_dc(b, dc):
        if True:
            di = dc % 2
            lhsT = dtT.ap[:, dc * P:(dc + 1) * P]
            ev_dt = None
            for j in range(4):
                rhs = dr_sb.ap[:, b * T + j * 512:b * T + (j + 1) * 512]
                ev_dt = e_mm(ps_t.ap[:, j * 512:(j + 1) * 512], lhsT, rhs,
                             True, True,
                             deps=dr_sb.wdeps() + dtT.wdeps() + ps_t.wdeps(),
                             inc="pe")
                dr_sb.used(ev_dt)
                dtT.used(ev_dt)
            ps_t.wrote(ev_dt)
            dcol = cst.ap[:, dc:dc + 1]
            dl = dbuf[di]
            ev1 = e_act(dl.ap, ps_t.ap, AF.Abs, bias=dcol,
                        deps=ps_t.wdeps() + dl.wdeps() + cst.wdeps(),
                        inc="act")
            ps_t.used(ev1)
            cst.used(ev1)
            ev2 = e_act(dl.ap, dl.ap, AF.Exp, scale=-1.0, deps=[ev1],
                        inc="act")
            ev3 = e_act(dl.ap, dl.ap, AF.Ln, bias=1.0, deps=[ev2], inc="act")
            ev4 = e_act(spbuf.ap, ps_t.ap, AF.Relu, bias=dcol,
                        deps=ps_t.wdeps() + spbuf.wdeps(), inc="act")
            ps_t.used(ev4)
            spbuf.wrote(ev4)
            ev_d = e_tt("dve", dl.ap, dl.ap, spbuf.ap, ALU.add,
                        deps=[ev3, ev4], inc="dve")
            spbuf.used(ev_d)
            dl.wrote(ev_d)
            xl = xnsb[di]
            ev_xl = e_dma("sp", xl.ap, xn_dram[b, dc * P:(dc + 1) * P, :],
                          deps=xl.wdeps() + [ev_xn_store.get(b)])
            xl.wrote(ev_xl)
            ub = ubuf[di]
            ev_u = e_tt("dve", ub.ap, dl.ap, xl.ap, ALU.mult,
                        deps=dl.wdeps() + xl.wdeps() + ub.wdeps(),
                        inc="dve")
            dl.used(ev_u)
            xl.used(ev_u)
            ub.wrote(ev_u)
            lnl = lnsb[di]
            ev_ld = e_dma("sp", lnl.ap, ln_dram[b, dc * P:(dc + 1) * P, :],
                          deps=lnl.wdeps() + [ev_ln_store.get(b)])
            lnl.wrote(ev_ld)

            stack = []

            def emit_w(k):
                hb = hpair[k % 2]
                Cb = Cpair[k % 2]
                wtgt = wpair[k % 2]
                ev_w = e_tt("dve", wtgt.ap, hb.ap, Cb.ap, ALU.mult,
                            deps=hb.wdeps() + Cb.wdeps() + wtgt.wdeps(),
                            inc="dve")
                hb.used(ev_w)
                Cb.used(ev_w)
                wtgt.wrote(ev_w)
                ev_wN[k] = ev_w

            def emit_pair(k):
                wp = wpair[k % 2]
                if k < NST // 2 - 1:
                    tb = free_tree.pop()
                    ev_sg = e_tt("dve", tb.ap, wp.ap[:, 0:T],
                                 wp.ap[:, T:2 * T], ALU.add,
                                 deps=wp.wdeps() + tb.wdeps(), inc="dve")
                    wp.used(ev_sg)
                    tb.wrote(ev_sg)
                    stack.append([0, tb])
                    while len(stack) >= 2 and stack[-1][0] == stack[-2][0]:
                        l2, b2 = stack.pop()
                        l1, b1 = stack.pop()
                        evm = e_tt("dve", b1.ap, b1.ap, b2.ap, ALU.add,
                                   deps=b1.wdeps() + b2.wdeps(), inc="dve")
                        b1.wrote(evm)
                        b2.used(evm)
                        free_tree.append(b2)
                        stack.append([l1 + 1, b1])
                else:
                    l1, b1 = stack[-1]
                    evm = e_tt("dve", b1.ap, b1.ap, wp.ap[:, 0:T], ALU.add,
                               deps=b1.wdeps() + wp.wdeps(), inc="dve")
                    b1.wrote(evm)
                    evm = e_tt("dve", b1.ap, b1.ap, wp.ap[:, T:2 * T],
                               ALU.add, deps=b1.wdeps() + wp.wdeps(),
                               inc="dve")
                    wp.used(evm)
                    b1.wrote(evm)

            for k in range(NST // 2):
                ns = (2 * k, 2 * k + 1)
                for n in ns:
                    Bb = Bbb[n % 4]
                    evB = e_dma("sp", Bb.ap,
                                bc_dram[b, n:n + 1, :].partition_broadcast(P),
                                deps=Bb.wdeps() + [ev_bc_store.get(b)])
                    Bb.wrote(evB)
                    Cb = Cpair[k % 2]
                    evC = e_dma("sp", Cb.ap[:, (n % 2) * T:(n % 2 + 1) * T],
                                bc_dram[b, NST + n:NST + n + 1, :]
                                .partition_broadcast(P),
                                deps=Cb.wdeps() + [ev_bc_store.get(b)])
                    Cb.wrote(evC)
                for n in ns:
                    ab = abuf[n % 2]
                    scl = A_sb.ap[:, dc * NST + n:dc * NST + n + 1]
                    ev_a = e_act(ab.ap, dl.ap, AF.Exp, scale=scl,
                                 deps=dl.wdeps() + ab.wdeps() +
                                 A_sb.wdeps(), inc="act")
                    dl.used(ev_a)
                    A_sb.used(ev_a)
                    ab.wrote(ev_a)
                for n in ns:
                    qb = qbuf[n % 2]
                    Bb = Bbb[n % 4]
                    ev_q = e_tt("dve", qb.ap, ub.ap, Bb.ap, ALU.mult,
                                deps=ub.wdeps() + Bb.wdeps() + qb.wdeps(),
                                inc="dve")
                    ub.used(ev_q)
                    Bb.used(ev_q)
                    qb.wrote(ev_q)
                if k >= 1:
                    emit_w(k - 1)
                if k >= 2:
                    emit_pair(k - 2)
                for n in ns:
                    ab = abuf[n % 2]
                    qb = qbuf[n % 2]
                    hb = hpair[k % 2]
                    ev_sc = e_scan(hb.ap[:, (n % 2) * T:(n % 2 + 1) * T],
                                   ab.ap, qb.ap,
                                   deps=ab.wdeps() + qb.wdeps() +
                                   hb.wdeps(), inc="dve")
                    ab.used(ev_sc)
                    qb.used(ev_sc)
                    hb.wrote(ev_sc)
                    ev_scanN[n] = ev_sc
            emit_w(NST // 2 - 1)
            emit_pair(NST // 2 - 2)
            emit_pair(NST // 2 - 1)
            while len(stack) > 1:
                l2, b2 = stack.pop()
                l1, b1 = stack.pop()
                evm = e_tt("dve", b1.ap, b1.ap, b2.ap, ALU.add,
                           deps=b1.wdeps() + b2.wdeps(), inc="dve")
                b1.wrote(evm)
                b2.used(evm)
                free_tree.append(b2)
                stack.append([l1 + 1, b1])
            _, ybf = stack.pop()
            lnl = lnsb[di]
            ev_z = e_stt("dve", zbuf.ap, lnl.ap, cst.ap[:, 2 * DCN + dc:
                                                        2 * DCN + dc + 1],
                         ybf.ap, ALU.mult, ALU.add,
                         deps=lnl.wdeps() + ybf.wdeps() + zbuf.wdeps() +
                         cst.wdeps(), inc="dve")
            lnl.used(ev_z)
            ybf.used(ev_z)
            cst.used(ev_z)
            zbuf.wrote(ev_z)
            free_tree.append(ybf)
            ev_zs = e_dma("sp", z_dram[b, dc * P:(dc + 1) * P, :], zbuf.ap,
                          deps=zbuf.wdeps())
            zbuf.used(ev_zs)
            ev_z_store[b] = ev_zs

    pl.mark("phS_end")
    ztb = (q0, q1, Bb2, Bb3)
    lnstage = (Bb0, Bb1, lns, tree[4])
    afb = (wb0, wb1, tree[0], tree[1])
    lwb = (tree[2], tree[3])
    x1b = spbuf
    xrl = (d0, d1)
    sqp = u0
    ln1b = u1
    midb = [Buf(w0p.ap[0:LR, i * TT:(i + 1) * TT]) for i in range(2)]
    midgb = [Buf(w0p.ap[0:LR, (2 + i) * TT:(3 + i) * TT]) for i in range(2)]
    gbb = [Buf(w0p.ap[0:LR, (4 + i) * TT:(5 + i) * TT]) for i in range(2)]
    gateb = Buf(w0p.ap[:, 6 * TT:6 * TT + 32])
    gTs = Buf(w0p.ap[0:32, 6 * TT + 32:6 * TT + 32 + P])
    mpack = Buf(w1p.ap[:, 0:TT])
    lgb = Buf(a1.ap[0:NEXP, 0:TT])
    subs = {"w0p": (w0p, midb + midgb + gbb + [gateb, gTs]),
            "w1p": (w1p, [mpack]),
            "a1": (a1, [lgb])}

    def parent_sync(name):
        par, kids = subs[name]
        for k in kids:
            k.ready = par.ready
            k.consumers = list(par.consumers)

    for nm in subs:
        parent_sync(nm)

    def emit_P_tt(b, tt):
        if True:
            t0 = tt * TT
            tdi = b * NTT + tt
            for qtr in range(4):
                zt = ztb[qtr]
                ev = e_dma(
                    "sp", zt.ap.rearrange("p (kc t) -> p kc t", kc=4),
                    z_dram[b].rearrange("(kc p) t -> p kc t", p=P)
                    [:, qtr * 4:(qtr + 1) * 4, t0:t0 + TT],
                    deps=zt.wdeps() + [ev_z_store.get(b)])
                zt.wrote(ev)
            for mc in range(DCN):
                lw = lwb[mc % 2]
                ev_lw = e_dma("sp",
                              lw.ap.rearrange("p (kc m) -> p kc m", kc=DCN),
                              opT_dram[mc].rearrange("kc p m -> p kc m"),
                              deps=lw.wdeps())
                lw.wrote(ev_lw)
                psd = (ps_a0, ps_a1)[mc % 2]
                evm = None
                for kc in range(DCN):
                    zt = ztb[kc // 4]
                    rhs = zt.ap[:, (kc % 4) * TT:(kc % 4 + 1) * TT]
                    evm = e_mm(psd.ap, lw.ap[:, kc * P:(kc + 1) * P], rhs,
                               kc == 0, kc == DCN - 1,
                               deps=zt.wdeps() + lw.wdeps() + psd.wdeps(),
                               inc="pe")
                    zt.used(evm)
                    lw.used(evm)
                psd.wrote(evm)
                af = afb[mc // 4]
                ev = e_act(af.ap[:, (mc % 4) * TT:(mc % 4 + 1) * TT], psd.ap,
                           AF.Identity, deps=psd.wdeps() + af.wdeps(), inc="act")
                psd.used(ev)
                af.wrote(ev)
            for s in range(NSUB):
                si = s % 2
                xr = xrl[si]
                ev_x = e_dma("sp", xr.ap,
                             x_in[b, t0 + s * P:t0 + s * P + P, :],
                             deps=xr.wdeps())
                xr.wrote(ev_x)
                evt = None
                for mc in range(DCN):
                    af = afb[mc // 4]
                    src = af.ap[:, (mc % 4) * TT + s * P:
                                (mc % 4) * TT + (s + 1) * P]
                    evt = e_tr(ps_bt.ap[:, mc * P:(mc + 1) * P], src, idb.ap,
                               deps=af.wdeps() + ps_bt.wdeps() + idb.wdeps(),
                               inc="pe")
                    af.used(evt)
                    idb.used(evt)
                ps_bt.wrote(evt)
                ev_x1 = e_tt("dve", x1b.ap, xr.ap, ps_bt.ap, ALU.add,
                             deps=xr.wdeps() + ps_bt.wdeps() + x1b.wdeps(),
                             inc="dve")
                xr.used(ev_x1)
                ps_bt.used(ev_x1)
                x1b.wrote(ev_x1)
                ev_sp = e_dma("sp", x1_dram[b, t0 + s * P:t0 + s * P + P, :],
                              x1b.ap, deps=x1b.wdeps())
                x1b.used(ev_sp)
                layer_norm_tm(x1b, ln1b, 0)
                stage_transpose(ln1b, lnstage, s)
            evm = None
            for kc in range(DCN):
                stg = lnstage[kc // 4]
                rhs = stg.ap[:, (kc % 4) * TT:(kc % 4 + 1) * TT]
                evm = e_mm(ps_a0.ap[0:NEXP, :],
                           rT.ap[:, kc * NEXP:(kc + 1) * NEXP], rhs,
                           kc == 0, kc == DCN - 1,
                           deps=stg.wdeps() + rT.wdeps() + ps_a0.wdeps(),
                           inc="pe")
                stg.used(evm)
                rT.used(evm)
            ps_a0.wrote(evm)
            parent_sync("a1")
            ev_lg = e_act(lgb.ap, ps_a0.ap[0:NEXP, :], AF.Identity, bias=rb_c.ap,
                          deps=ps_a0.wdeps() + lgb.wdeps() + rb_c.wdeps(),
                          inc="act")
            ps_a0.used(ev_lg)
            rb_c.used(ev_lg)
            lgb.wrote(ev_lg)
            a1.wrote(ev_lg)
            for s in range(NSUB):
                evt = e_tr(ps_t.ap[0:P, s * NEXP:(s + 1) * NEXP],
                           lgb.ap[0:NEXP, s * P:(s + 1) * P],
                           idf.ap[0:NEXP, 0:NEXP],
                           deps=lgb.wdeps() + ps_t.wdeps() + idf.wdeps(),
                           inc="pe")
                lgb.used(evt)
                a1.used(evt)
                idf.used(evt)
                ps_t.wrote(evt)
                eve = e_copy("dve",
                             gsc.ap[:, 32 + s * NEXP:32 + (s + 1) * NEXP],
                             ps_t.ap[0:P, s * NEXP:(s + 1) * NEXP],
                             deps=[evt] + gsc.wdeps(), inc="dve")
                ps_t.used(eve)
                gsc.wrote(eve)
            for s in range(NSUB):
                lg = gsc.ap[:, 32 + s * NEXP:32 + (s + 1) * NEXP]
                m1 = gsc.ap[:, 64:65]
                nm1 = gsc.ap[:, 65 + s:66 + s]
                m2 = gsc.ap[:, 69:70]
                ddc = gsc.ap[:, 70 + s:71 + s]
                dnc = gsc.ap[:, 74:75]
                ivc = gsc.ap[:, 75:76]
                mask1 = gsc.ap[:, 80:88]
                lg2 = gsc.ap[:, 96:104]
                mask2 = gsc.ap[:, 104:112]
                ev1 = e_red(m1, lg, ALU.max, deps=gsc.wdeps() + stat.wdeps(),
                            inc="dve")
                evn = e_ts("dve", nm1, m1, -1.0, ALU.mult, deps=[ev1],
                           inc="dve", strict=True)
                ev3 = e_ts("dve", mask1, lg, m1, ALU.is_equal, deps=[ev1],
                           inc="dve", strict=True)
                ev4 = e_stt("dve", lg2, mask1, -1e30, lg, ALU.mult, ALU.add,
                            deps=[ev3], inc="dve", strict=True)
                ev5 = e_red(m2, lg2, ALU.max, deps=[ev4], inc="dve",
                            strict=True)
                ev6 = e_ts("dve", mask2, lg2, m2, ALU.is_equal, deps=[ev5],
                           inc="dve", strict=True)
                ev7 = e_act(lg, lg, AF.Exp, bias=nm1, deps=[ev5, evn],
                            inc="act", strict=True)
                ev8 = e_tt("dve", mask1, mask1, mask2, ALU.add,
                           deps=[ev6, ev7], inc="dve", strict=True)
                ev9 = e_tt("dve", ddc, m2, m1, ALU.subtract, deps=[ev5],
                           inc="dve", strict=True)
                ev10 = e_act(dnc, ddc, AF.Exp, deps=[ev9], inc="act",
                             strict=True)
                ev11 = e_ts("dve", dnc, dnc, 1.0, ALU.add, deps=[ev10],
                            inc="dve", strict=True)
                ev12 = e_recip(ivc, dnc, deps=[ev11], inc="dve", strict=True)
                ev13 = e_tt("dve", lg, lg, mask1, ALU.mult, deps=[ev7, ev8],
                            inc="dve", strict=True)
                gcols = gateb.ap.rearrange(
                    "p (e s2) -> p s2 e", s2=4)[:, s, :]
                ev14 = e_ts("dve", gcols, lg,
                            ivc, ALU.mult,
                            deps=[ev12, ev13] + gateb.wdeps(), inc="dve",
                            strict=True)
                gateb.wrote(ev14)
                w0p.wrote(ev14)
                gsc.wrote(ev14)
            evt = e_tr(ps_bt.ap[0:32, 0:P], gateb.ap, idb.ap,
                       deps=gateb.wdeps() + ps_bt.wdeps() + idb.wdeps(),
                       inc="pe")
            gateb.used(evt)
            w0p.used(evt)
            idb.used(evt)
            ps_bt.wrote(evt)
            eve = e_copy("dve", gTs.ap, ps_bt.ap[0:32, 0:P],
                         deps=[evt] + gTs.wdeps(), inc="dve")
            ps_bt.used(eve)
            gTs.wrote(eve)
            w0p.wrote(eve)
            ev_gst = e_dma("sp", g_dram[tdi], gTs.ap,
                           deps=gTs.wdeps())
            gTs.used(ev_gst)
            w0p.used(ev_gst)
            parent_sync("w0p")
            parent_sync("w1p")
            ev_pack = None
            for e_ in range(NEXP):
                ei = e_ % 2
                evm = None
                for kc in range(DCN):
                    stg = lnstage[kc // 4]
                    rhs = stg.ap[:, (kc % 4) * TT:(kc % 4 + 1) * TT]
                    lhsT = laT.ap[:, (kc * NEXP + e_) * LR:
                                  (kc * NEXP + e_ + 1) * LR]
                    evm = e_mm(ps_a0.ap[0:LR, :], lhsT, rhs, kc == 0,
                               kc == DCN - 1,
                               deps=stg.wdeps() + laT.wdeps() +
                               ps_a0.wdeps(), inc="pe")
                    stg.used(evm)
                    laT.used(evm)
                ps_a0.wrote(evm)
                md = midb[ei]
                ev_md = e_act(md.ap, ps_a0.ap[0:LR, :], AF.Identity,
                              deps=ps_a0.wdeps() + md.wdeps(), inc="act")
                ps_a0.used(ev_md)
                md.wrote(ev_md)
                w0p.used(ev_md)
                gb = gbb[ei]
                gsrc = g_dram[tdi].rearrange(
                    "(e s2) j -> e (s2 j)", s2=4)[e_:e_ + 1, :]
                ev_gb = e_dma("sp", gb.ap,
                              gsrc.partition_broadcast(LR),
                              deps=gb.wdeps() + [ev_gst])
                gb.wrote(ev_gb)
                w0p.used(ev_gb)
                mg = midgb[ei]
                ev_mg = e_tt("dve", mg.ap, md.ap, gb.ap, ALU.mult,
                             deps=md.wdeps() + gb.wdeps() + mg.wdeps(),
                             inc="dve")
                md.used(ev_mg)
                gb.used(ev_mg)
                mg.wrote(ev_mg)
                w0p.used(ev_mg)
                ev_pack = e_mm(ps_a1.ap, selb.ap[:, e_ * P:(e_ + 1) * P],
                               mg.ap, e_ == 0, e_ == NEXP - 1,
                               deps=mg.wdeps() + selb.wdeps() +
                               ps_a1.wdeps(), inc="pe")
                mg.used(ev_pack)
                selb.used(ev_pack)
            ps_a1.wrote(ev_pack)
            ev_mp = e_act(mpack.ap, ps_a1.ap, AF.Identity,
                          deps=ps_a1.wdeps() + mpack.wdeps(), inc="act")
            ps_a1.used(ev_mp)
            mpack.wrote(ev_mp)
            w1p.wrote(ev_mp)
            for mc in range(DCN):
                lw = lwb[mc % 2]
                ev_lw = e_dma("sp",
                              lw.ap.rearrange("p (kc m) -> p kc m", kc=DCN),
                              baT_dram[mc].rearrange("kc p m -> p kc m"),
                              deps=lw.wdeps())
                lw.wrote(ev_lw)
                psd = (ps_a0, ps_a1)[mc % 2]
                evm = e_mm(psd.ap, lbT.ap[:, mc * P:(mc + 1) * P], mpack.ap,
                           True, False,
                           deps=mpack.wdeps() + lbT.wdeps() + psd.wdeps(),
                           inc="pe")
                mpack.used(evm)
                w1p.used(evm)
                lbT.used(evm)
                for kc in range(DCN):
                    stg = lnstage[kc // 4]
                    rhs = stg.ap[:, (kc % 4) * TT:(kc % 4 + 1) * TT]
                    evm = e_mm(psd.ap, lw.ap[:, kc * P:(kc + 1) * P], rhs,
                               False, kc == DCN - 1,
                               deps=stg.wdeps() + lw.wdeps(),
                               inc="pe")
                    stg.used(evm)
                    lw.used(evm)
                psd.wrote(evm)
                af = afb[mc // 4]
                ev = e_act(af.ap[:, (mc % 4) * TT:(mc % 4 + 1) * TT], psd.ap,
                           AF.Identity, bias=cst.ap[:, DCN + mc:DCN + mc + 1],
                           deps=psd.wdeps() + af.wdeps() + cst.wdeps(),
                           inc="act")
                psd.used(ev)
                cst.used(ev)
                af.wrote(ev)
            for s in range(NSUB):
                si = s % 2
                xr = xrl[si]
                ev_x1r = e_dma("sp", xr.ap,
                               x1_dram[b, t0 + s * P:t0 + s * P + P, :],
                               deps=xr.wdeps())
                xr.wrote(ev_x1r)
                evt = None
                for mc in range(DCN):
                    af = afb[mc // 4]
                    src = af.ap[:, (mc % 4) * TT + s * P:
                                (mc % 4) * TT + (s + 1) * P]
                    evt = e_tr(ps_bt.ap[:, mc * P:(mc + 1) * P], src, idb.ap,
                               deps=af.wdeps() + ps_bt.wdeps() + idb.wdeps(),
                               inc="pe")
                    af.used(evt)
                    idb.used(evt)
                ps_bt.wrote(evt)
                ev_o = e_tt("dve", xr.ap, xr.ap, ps_bt.ap, ALU.add,
                            deps=xr.wdeps() + ps_bt.wdeps(), inc="dve")
                ps_bt.used(ev_o)
                xr.wrote(ev_o)
                ev_os = e_dma("sp", out_d[b, t0 + s * P:t0 + s * P + P, :],
                              xr.ap, deps=xr.wdeps(), inc="st")
                xr.used(ev_os)

    for dc in range(DCN):
        emit_S_dc(0, dc)
    pl.mark("b0S_end")
    for dc in range(DCN):
        emit_S_dc(1, dc)
        if dc % 4 == 1:
            emit_P_tt(0, dc // 4)
    pl.mark("b1S_end")
    for tt in range(NTT):
        emit_P_tt(1, tt)
    pl.mark("b1P_end")

    n_out = pl.count["st"]

    def fin(e):
        return e.nop()
    pl.emit("sp", fin, deps=[("st", n_out)], inc=None)

    with (
        nc.semaphore("s_sp") as s_sp,
        nc.semaphore("s_gp") as s_gp,
        nc.semaphore("s_st") as s_st,
        nc.semaphore("s_act") as s_act,
        nc.semaphore("s_dve") as s_dve,
        nc.semaphore("s_pool") as s_pool,
        nc.semaphore("s_pe") as s_pe,
        nc.semaphore("s_rdy") as s_rdy,
        nc.semaphore("s_rdyg") as s_rdyg,
        nc.Block() as block,
    ):
        sems = dict(sp=s_sp, gp=s_gp, st=s_st, act=s_act, dve=s_dve,
                    pool=s_pool, pe=s_pe, rdy=s_rdy, rdyg=s_rdyg)

        @block.sync
        def _(eng):
            pl.replay("sp", eng, sems, nc)

        @block.scalar
        def _(eng):
            pl.replay("act", eng, sems, nc)

        @block.vector
        def _(eng):
            pl.replay("dve", eng, sems, nc)

        @block.gpsimd
        def _(eng):
            pl.replay("gp", eng, sems, nc)
            pl.replay("pool", eng, sems, nc)

        @block.tensor
        def _(eng):
            pl.replay("pe", eng, sems, nc)

    for cm in reversed(ctxs):
        cm.__exit__(None, None, None)

    return nc


_NC_CACHE = None


def _consts():
    idb = np.eye(P, dtype=ml_dtypes.bfloat16)
    idf = np.eye(P, dtype=np.float32)
    oneb = np.ones((1, P), dtype=ml_dtypes.bfloat16)
    selb = np.zeros((LR, NEXP * P), dtype=ml_dtypes.bfloat16)
    for e in range(NEXP):
        for k in range(LR):
            selb[k, e * P + LR * e + k] = 1.0
    return idb, idf, oneb, selb


def kernel(L0_attn_x, L0_attn_norm_w, L0_attn_norm_b, L0_attn_x_proj_w,
           L0_attn_dt_proj_w, L0_attn_dt_proj_b, L0_attn_A_log,
           L0_attn_D_param, L0_attn_out_proj_w, L0_ffn_base_w, L0_ffn_base_b,
           L0_ffn_router_w, L0_ffn_router_b, L0_ffn_lora_A, L0_ffn_lora_B):
    global _NC_CACHE
    if _NC_CACHE is None:
        _NC_CACHE = build_program()
    nc = _NC_CACHE

    idb, idf, oneb, selb = _consts()
    x = np.ascontiguousarray(L0_attn_x)
    common = dict(
        normw=L0_attn_norm_w, normb=L0_attn_norm_b, xproj=L0_attn_x_proj_w,
        dtw=L0_attn_dt_proj_w, dtb=L0_attn_dt_proj_b, alog=L0_attn_A_log,
        dpar=L0_attn_D_param, opw=L0_attn_out_proj_w, basew=L0_ffn_base_w,
        baseb=L0_ffn_base_b, routw=L0_ffn_router_w, routb=L0_ffn_router_b,
        loraA=L0_ffn_lora_A, loraB=L0_ffn_lora_B,
        idb=idb, idf=idf, oneb=oneb, selb=selb,
    )
    common = {k: np.ascontiguousarray(v) for k, v in common.items()}
    in_maps = []
    for c in range(8):
        m = dict(common)
        m["x"] = x[c * BC:(c + 1) * BC]
        in_maps.append(m)
    res = run_bass_kernel_spmd(nc, in_maps, core_ids=list(range(8)))
    return np.concatenate([r["out"] for r in res.results], axis=0)


if __name__ == "__main__":
    nc = build_program()
    n = sum(1 for _ in nc.all_instructions())
    print("built ok, instructions:", n)


# version 14 (final)
# speedup vs baseline: 1.1462x; 1.1165x over previous
import numpy as np
import ml_dtypes
import concourse.bass as bass
import concourse.mybir as mybir
from concourse.bass_utils import run_bass_kernel_spmd

F32 = mybir.dt.float32
BF16 = mybir.dt.bfloat16
AF = mybir.ActivationFunctionType
ALU = mybir.AluOpType
AX = mybir.AxisListType

P = 128
D = 2048
T = 2048
BC = 2
DCN = 16
NST = 64
RDT = 128
TT = 512
NTT = T // TT
NSUB = TT // P
EPS = 1e-5
NEXP = 8
LR = 16

ENGINES = ("sp", "act", "dve", "pool", "pe", "gp")
DMA_STEP = {"sp": 16, "gp": 16, "st": 16}


class Plan:
    def __init__(self):
        self.ops = {e: [] for e in ENGINES}
        self.count = {s: 0 for s in
                      ("sp", "gp", "st", "act", "dve", "pool", "pe",
                       "rdy", "rdyg")}
        self.lastwait = {e: {} for e in ENGINES}
        self.marks = []
        self.signals = {}

    def mark(self, name):
        self.marks.append((name, {e: len(self.ops[e]) for e in ENGINES}))

    DMA_OWNER = {"sp": "sp", "st": "sp", "gp": "gp"}
    RDY = {"sp": "rdy", "st": "rdy", "gp": "rdyg"}

    def _signal(self, qsem):
        rdy = self.RDY[qsem]
        covered = self.count[qsem]
        sigs = self.signals.setdefault(qsem, [])
        if sigs and sigs[-1][0] >= covered:
            return sigs[-1][1]
        self.count[rdy] += 1
        ev = (rdy, self.count[rdy])
        stream = "sp" if self.DMA_OWNER[qsem] == "sp" else "gp"
        self.ops[stream].append((("SIG", rdy), [(qsem, covered)], ev))
        sigs.append((covered, ev))
        return ev

    def emit(self, eng, fn, deps=(), inc=None, strict=False):
        waits = []
        for dep in deps:
            if dep is None:
                continue
            sem, val = dep
            if sem in DMA_STEP:
                owner = self.DMA_OWNER[sem]
                if (eng == owner) or (eng == "pool" and owner == "gp"):
                    val = self.count[sem]
                else:
                    ev2 = None
                    for cov, sev in self.signals.get(sem, []):
                        if cov >= val:
                            ev2 = sev
                            break
                    if ev2 is None:
                        ev2 = self._signal(sem)
                    sem, val = ev2
            if sem == eng and sem not in DMA_STEP and not strict:
                continue
            if self.lastwait[eng].get(sem, 0) >= val:
                continue
            self.lastwait[eng][sem] = val
            waits.append((sem, val))
        ev = None
        if inc is not None:
            self.count[inc] += DMA_STEP.get(inc, 1)
            ev = (inc, self.count[inc])
        self.ops[eng].append((fn, waits, ev))
        return ev

    def replay(self, eng, engobj, sems, nc=None):
        bounds = [(nm, idx[eng]) for nm, idx in self.marks]
        bi = 0
        import contextlib
        ctx = contextlib.nullcontext()
        opened = None
        for i, (fn, waits, ev) in enumerate(self.ops[eng]):
            while bi < len(bounds) and bounds[bi][1] == i:
                if opened is not None:
                    opened.__exit__(None, None, None)
                    opened = None
                if nc is not None:
                    opened = nc.named_scope(bounds[bi][0])
                    opened.__enter__()
                bi += 1
            for sem, val in waits:
                engobj.wait_ge(sems[sem], val)
            if isinstance(fn, tuple) and fn[0] == "SIG":
                engobj.sem_inc(sems[fn[1]], 1)
                continue
            inst = fn(engobj)
            if ev is not None:
                sem, _ = ev
                inst.then_inc(sems[sem], DMA_STEP.get(sem, 1))
        if opened is not None:
            opened.__exit__(None, None, None)


class Buf:
    def __init__(self, ap):
        self.ap = ap
        self.ready = None
        self.consumers = []

    def wdeps(self):
        return [self.ready] + self.consumers

    def wrote(self, ev):
        if ev is not None:
            self.ready = ev
        self.consumers = []

    def used(self, ev):
        if ev is not None:
            self.consumers.append(ev)


def build_program():
    nc = bass.Bass("TRN2", target_bir_lowering=False)

    x_in = nc.dram_tensor("x", [BC, T, D], F32, kind="ExternalInput").ap()
    wi = {}
    for nm, shp, dt in [
            ("normw", [D], F32), ("normb", [D], F32), ("xproj", [256, D], F32),
            ("dtw", [D, RDT], F32), ("dtb", [D], F32), ("alog", [D, NST], F32),
            ("dpar", [D], F32), ("opw", [D, D], F32), ("basew", [D, D], F32),
            ("baseb", [D], F32), ("routw", [NEXP, D], F32),
            ("routb", [NEXP], F32), ("loraA", [NEXP, D, LR], F32),
            ("loraB", [NEXP, LR, D], F32), ("idb", [P, P], BF16),
            ("idf", [P, P], F32), ("oneb", [1, P], BF16),
            ("selb", [LR, NEXP * P], BF16)]:
        wi[nm] = nc.dram_tensor(nm, shp, dt, kind="ExternalInput").ap()

    out_d = nc.dram_tensor("out", [BC, T, D], F32, kind="ExternalOutput").ap()

    xn_dram = nc.dram_tensor("xn_sc", [BC, D, T], BF16).ap()
    ln_dram = nc.dram_tensor("ln_sc", [BC, D, T], BF16).ap()
    z_dram = nc.dram_tensor("z_sc", [BC, D, T], BF16).ap()
    bc_dram = nc.dram_tensor("bc_sc", [BC, P, T], BF16).ap()
    x1_dram = nc.dram_tensor("x1_sc", [BC, T, D], F32).ap()
    opT_dram = nc.dram_tensor("opT_sc", [DCN, DCN, P, P], BF16).ap()
    baT_dram = nc.dram_tensor("baT_sc", [DCN, DCN, P, P], BF16).ap()
    g_dram = nc.dram_tensor("g_sc", [BC * NTT, 32, P], BF16).ap()

    pl = Plan()
    ctxs = []

    def sb(name, shape, dtype):
        cm = nc.sbuf_tensor(name, shape, dtype)
        t = cm.__enter__()
        ctxs.append(cm)
        return Buf(t[:])

    def psb(name, shape, dtype):
        cm = nc.psum_tensor(name, shape, dtype)
        t = cm.__enter__()
        ctxs.append(cm)
        return Buf(t[:])

    xpT = sb("xpT", [P, 32 * P], BF16)
    dtT = sb("dtT", [P, DCN * P], BF16)
    A_sb = sb("A_sb", [P, DCN * NST], F32)
    laT = sb("laT", [P, DCN * NEXP * LR], BF16)
    lbT = sb("lbT", [P, D], BF16)
    selb = sb("selb_s", [LR, NEXP * P], BF16)
    rT = sb("rT", [P, DCN * NEXP], BF16)
    wbB = sb("wbB", [P, D], BF16)
    bbB = sb("bbB", [P, D], BF16)
    cst = sb("cst", [P, 4 * DCN], F32)
    rb_c = sb("rb_c", [NEXP, 1], F32)
    idb = sb("idb_s", [P, P], BF16)
    idf = sb("idf_s", [P, 32], F32)
    oneb = sb("oneb_s", [1, P], BF16)
    dr_sb = sb("dr_sb", [P, BC * T], BF16)
    a0 = sb("a0", [P, T], F32)
    a1 = sb("a1", [P, T], F32)
    q0 = sb("q0", [P, T], BF16)
    q1 = sb("q1", [P, T], BF16)
    Bb0 = sb("Bb0", [P, 2 * T], BF16)
    Bb1 = sb("Bb1", [P, 2 * T], BF16)
    Bb2 = sb("Bb2", [P, T], BF16)
    Bb3 = sb("Bb3", [P, T], BF16)
    Cb0 = sb("Cb0", [P, 2 * T], BF16)
    Cb1 = sb("Cb1", [P, 2 * T], BF16)
    h0p = sb("h0p", [P, 2 * T], BF16)
    h1p = sb("h1p", [P, 2 * T], BF16)
    w0p = sb("w0p", [P, 2 * T], BF16)
    w1p = sb("w1p", [P, 2 * T], BF16)
    wb0 = sb("wb0", [P, T], BF16)
    wb1 = sb("wb1", [P, T], BF16)
    tree = [sb(f"lvl{i}", [P, T], BF16) for i in range(5)]
    spbuf = sb("spbuf", [P, T], F32)
    d0 = sb("d0", [P, T], F32)
    d1 = sb("d1", [P, T], F32)
    u0 = sb("u0", [P, T], BF16)
    u1 = sb("u1", [P, T], BF16)
    xns = sb("xns", [P, T], BF16)
    lns = sb("lns", [P, T], BF16)
    stat = sb("stat", [P, 16], F32)
    gsc = sb("gsc", [P, 112], F32)
    ps_a0 = psb("ps_a0", [P, 512], F32)
    ps_a1 = psb("ps_a1", [P, 512], F32)
    ps_t = psb("ps_t", [P, 2048], F32)
    ps_bt = psb("ps_bt", [P, 2048], BF16)

    def e_dma(q, out, in_, deps=(), inc="sp"):
        def fn(e):
            return e.dma_start(out=out, in_=in_)
        return pl.emit(q, fn, deps, inc)

    def e_tt(eng, out, in0, in1, op, deps=(), inc=None, strict=False):
        def fn(e):
            return e.tensor_tensor(out=out, in0=in0, in1=in1, op=op)
        return pl.emit(eng, fn, deps, inc, strict)

    def e_ts(eng, out, in0, scalar, op, deps=(), inc=None, strict=False):
        def fn(e):
            return e.tensor_single_scalar(out=out, in_=in0, scalar=scalar,
                                          op=op)
        return pl.emit(eng, fn, deps, inc, strict)

    def e_stt(eng, out, in0, scalar, in1, op0, op1, deps=(), inc=None,
              strict=False):
        def fn(e):
            return e.scalar_tensor_tensor(out=out, in0=in0, scalar=scalar,
                                          in1=in1, op0=op0, op1=op1)
        return pl.emit(eng, fn, deps, inc, strict)

    def e_act(out, in_, func, bias=0.0, scale=1.0, deps=(), inc=None,
              strict=False):
        def fn(e):
            return e.activation(out=out, in_=in_, func=func, bias=bias,
                                scale=scale)
        return pl.emit("act", fn, deps, inc, strict)

    def e_copy(eng, out, in_, deps=(), inc=None):
        def fn(e):
            return e.tensor_copy(out=out, in_=in_)
        return pl.emit(eng, fn, deps, inc)

    def e_red(out, in_, op, deps=(), inc=None, strict=False):
        def fn(e):
            return e.tensor_reduce(out=out, in_=in_, axis=AX.X, op=op)
        return pl.emit("dve", fn, deps, inc, strict)

    def e_mm(out, lhsT, rhs, start, stop, deps=(), inc=None):
        def fn(e):
            return e.matmul(out=out, lhsT=lhsT, rhs=rhs, start=start,
                            stop=stop)
        return pl.emit("pe", fn, deps, inc)

    def e_tr(out, in_, ident, deps=(), inc=None):
        def fn(e):
            return e.transpose(out=out, in_=in_, identity=ident)
        return pl.emit("pe", fn, deps, inc)

    def e_scan(out, data0, data1, deps=(), inc=None):
        def fn(e):
            return e.tensor_tensor_scan(out=out, data0=data0, data1=data1,
                                        initial=0.0, op0=ALU.mult,
                                        op1=ALU.add)
        return pl.emit("dve", fn, deps, inc)

    def e_recip(out, in_, deps=(), inc=None, strict=False):
        def fn(e):
            return e.reciprocal(out=out, in_=in_)
        return pl.emit("dve", fn, deps, inc, strict)

    for bufb, src in ((idb, wi["idb"]), (idf, wi["idf"][:, 0:32]),
                      (oneb, wi["oneb"]), (selb, wi["selb"])):
        ev = e_dma("sp", bufb.ap, src)
        bufb.wrote(ev)

    xp_rows = (Bb0, Bb1)
    for c in range(2):
        ev = e_dma("gp", xp_rows[c].ap[:, 0:D],
                   wi["xproj"][c * P:(c + 1) * P, :], inc="gp")
        xp_rows[c].wrote(ev)
    dt_rows = Cb0
    ev = e_dma("gp",
               dt_rows.ap[:, 0:DCN * RDT].rearrange("p (c r) -> p c r",
                                                    c=DCN),
               wi["dtw"].rearrange("(c p) r -> p c r", p=P), inc="gp")
    dt_rows.wrote(ev)
    al_buf = d0
    ev = e_dma("gp",
               al_buf.ap[:, 0:DCN * NST].rearrange("p (c n) -> p c n", c=DCN),
               wi["alog"].rearrange("(c p) n -> p c n", p=P), inc="gp")
    al_buf.wrote(ev)
    rw_rows = Cb1
    ev = e_dma("gp", rw_rows.ap[0:NEXP, 0:D], wi["routw"], inc="gp")
    rw_rows.wrote(ev)
    for e_ in range(NEXP):
        ev = e_dma("gp",
                   laT.ap.rearrange("p (c e r) -> p c e r", c=DCN,
                                    e=NEXP)[:, :, e_, :],
                   wi["loraA"][e_].rearrange("(c p) r -> p c r", p=P),
                   inc="gp")
    laT.wrote(ev)
    ev = e_dma("gp", lbT.ap, wi["loraB"].rearrange("e r d -> (e r) d"),
               inc="gp")
    lbT.wrote(ev)
    nwrow = q0
    ev = e_dma("gp", nwrow.ap[0:1, 0:D], wi["normw"][None, :], inc="gp")
    nwrow.wrote(ev)
    nbrow = q1
    ev = e_dma("gp", nbrow.ap[0:1, 0:D], wi["normb"][None, :], inc="gp")
    nbrow.wrote(ev)
    cols3 = d1
    ev = e_dma("gp", cols3.ap[0:DCN, 0:P],
               wi["dtb"].rearrange("(c p) -> c p", p=P), inc="gp")
    ev = e_dma("gp", cols3.ap[0:DCN, P:2 * P],
               wi["baseb"].rearrange("(c p) -> c p", p=P), inc="gp")
    ev = e_dma("gp", cols3.ap[0:DCN, 2 * P:3 * P],
               wi["dpar"].rearrange("(c p) -> c p", p=P), inc="gp")
    cols3.wrote(ev)
    ev = e_dma("gp", rb_c.ap, wi["routb"][:, None], inc="gp")
    rb_c.wrote(ev)

    ev = e_act(lbT.ap, lbT.ap, AF.Identity, scale=2.0, deps=lbT.wdeps(),
               inc="act")
    lbT.wrote(ev)
    ev = e_act(A_sb.ap, al_buf.ap[:, 0:DCN * NST], AF.Exp,
               deps=al_buf.wdeps() + A_sb.wdeps(), inc="act")
    al_buf.used(ev)
    ev = e_ts("dve", A_sb.ap, A_sb.ap, -1.0, ALU.mult, deps=[ev], inc="dve")
    A_sb.wrote(ev)

    def transpose_tile(src_ap, src_buf, dst_ap, dst_buf, ident=None,
                       pscols=P, psbuf=None, nrow=P):
        idbuf = ident or idb
        ps = psbuf or ps_bt
        id_ap = idbuf.ap if nrow == P else idbuf.ap[0:nrow, 0:nrow]
        evt = e_tr(ps.ap[:, 0:pscols] if nrow == P else
                   ps.ap[0:pscols, 0:nrow].ap if False else
                   ps.ap[:, 0:nrow][0:pscols, :] if False else
                   ps.ap[0:pscols, 0:nrow],
                   src_ap, id_ap,
                   deps=src_buf.wdeps() + ps.wdeps() + idbuf.wdeps(),
                   inc="pe")
        src_buf.used(evt)
        idbuf.used(evt)
        ps.wrote(evt)
        eve = e_copy("dve", dst_ap, ps.ap[0:pscols, 0:nrow],
                     deps=[evt] + dst_buf.wdeps(), inc="dve")
        ps.used(eve)
        dst_buf.wrote(eve)
        return eve

    for kc in range(DCN):
        for mc in range(2):
            transpose_tile(
                xp_rows[mc].ap[:, kc * P:(kc + 1) * P], xp_rows[mc],
                xpT.ap[:, (kc * 2 + mc) * P:(kc * 2 + mc + 1) * P], xpT)
    for kc in range(DCN):
        transpose_tile(dt_rows.ap[:, kc * RDT:(kc + 1) * RDT], dt_rows,
                       dtT.ap[:, kc * P:(kc + 1) * P], dtT)
    for kc in range(DCN):
        transpose_tile(rw_rows.ap[0:NEXP, kc * P:(kc + 1) * P], rw_rows,
                       rT.ap[:, kc * NEXP:(kc + 1) * NEXP], rT, nrow=NEXP)
    for i in range(3):
        transpose_tile(cols3.ap[0:DCN, i * P:(i + 1) * P], cols3,
                       cst.ap[:, i * DCN:(i + 1) * DCN], cst, ident=idf,
                       psbuf=ps_t, nrow=DCN)

    def bcast_row(src_ap, src_buf, dstb):
        last = None
        for c in range(4):
            evm = e_mm(ps_a0.ap, oneb.ap, src_ap[:, c * 512:(c + 1) * 512],
                       True, True,
                       deps=src_buf.wdeps() + oneb.wdeps() + ps_a0.wdeps(),
                       inc="pe")
            src_buf.used(evm)
            oneb.used(evm)
            ps_a0.wrote(evm)
            last = e_act(dstb.ap[:, c * 512:(c + 1) * 512], ps_a0.ap,
                         AF.Identity, deps=[evm] + dstb.wdeps(), inc="act")
            ps_a0.used(last)
        dstb.wrote(last)

    bcast_row(nwrow.ap[0:1, 0:D], nwrow, wbB)
    bcast_row(nbrow.ap[0:1, 0:D], nbrow, bbB)

    stage_w = wb0

    def big_wT(w_ap, dst_dram, rb0, rb1):
        for mc in range(DCN):
            rb = (rb0, rb1)[mc % 2]
            ev_row = e_dma("gp", rb.ap, w_ap[mc * P:(mc + 1) * P, :],
                           deps=rb.wdeps(), inc="gp")
            rb.wrote(ev_row)
            for kc in range(DCN):
                transpose_tile(rb.ap[:, kc * P:(kc + 1) * P], rb,
                               stage_w.ap[:, kc * P:(kc + 1) * P], stage_w)
            ev_st = e_dma(
                "sp", dst_dram[mc].rearrange("kc p m -> p kc m"),
                stage_w.ap.rearrange("p (kc m) -> p kc m", kc=DCN),
                deps=stage_w.wdeps())
            stage_w.used(ev_st)

    big_wT(wi["opw"], opT_dram, u0, u1)
    big_wT(wi["basew"], baT_dram, u0, u1)

    pl.mark("phW_end")
    xsub = (a0, a1)
    ln0b = u0
    sqb = u1
    t1b = xns
    xnb = Buf(Cb0.ap[:, 0:T])
    xn_stage = (Bb2, Bb3, q0, q1)
    ln_stage = (wb0, wb1, tree[0], tree[1])
    bc_stage = lns

    ev_bc_store = {}
    ev_xn_store = {}
    ev_ln_store = {}

    def layer_norm_tm(xbuf, outbuf, c0, scale_extra=None):
        st = stat.ap
        ev_mu = e_red(st[:, c0:c0 + 1], xbuf.ap, ALU.add,
                      deps=xbuf.wdeps() + stat.wdeps(), inc="dve")
        xbuf.used(ev_mu)
        ev_sq = e_act(sqb.ap, xbuf.ap, AF.Square,
                      deps=xbuf.wdeps() + sqb.wdeps(), inc="act")
        xbuf.used(ev_sq)
        sqb.wrote(ev_sq)
        ev_s2 = e_red(st[:, c0 + 1:c0 + 2], sqb.ap, ALU.add,
                      deps=sqb.wdeps(), inc="dve")
        sqb.used(ev_s2)
        e1_ = e_ts("dve", st[:, c0:c0 + 1], st[:, c0:c0 + 1], 1.0 / D,
                   ALU.mult, deps=[ev_mu], inc="dve", strict=True)
        e2_ = e_ts("dve", st[:, c0 + 1:c0 + 2], st[:, c0 + 1:c0 + 2],
                   1.0 / D, ALU.mult, deps=[ev_s2], inc="dve", strict=True)
        e3_ = e_tt("dve", st[:, c0 + 2:c0 + 3], st[:, c0:c0 + 1],
                   st[:, c0:c0 + 1], ALU.mult, deps=[e1_], inc="dve",
                   strict=True)
        e4_ = e_tt("dve", st[:, c0 + 1:c0 + 2], st[:, c0 + 1:c0 + 2],
                   st[:, c0 + 2:c0 + 3], ALU.subtract, deps=[e2_, e3_],
                   inc="dve", strict=True)
        ev_v = e_ts("dve", st[:, c0 + 1:c0 + 2], st[:, c0 + 1:c0 + 2], EPS,
                    ALU.add, deps=[e4_], inc="dve", strict=True)
        ev_l = e_act(st[:, c0 + 2:c0 + 3], st[:, c0 + 1:c0 + 2], AF.Ln,
                     deps=[ev_v], inc="act", strict=True)
        ev_r = e_act(st[:, c0 + 3:c0 + 4], st[:, c0 + 2:c0 + 3], AF.Exp,
                     scale=-0.5, deps=[ev_l], inc="act", strict=True)
        ev_b = e_stt("dve", st[:, c0 + 4:c0 + 5], st[:, c0:c0 + 1], -1.0,
                     st[:, c0 + 3:c0 + 4], ALU.mult, ALU.mult, deps=[ev_r],
                     inc="dve", strict=True)
        ev_o = e_act(outbuf.ap, xbuf.ap, AF.Identity, bias=st[:, c0 + 4:c0 + 5],
                     scale=st[:, c0 + 3:c0 + 4],
                     deps=[ev_b] + outbuf.wdeps(), inc="act")
        xbuf.used(ev_o)
        outbuf.wrote(ev_o)
        return ev_o

    def stage_transpose(srcb, stages, s):
        for qtr in range(4):
            evts = None
            for j in range(4):
                kc = qtr * 4 + j
                evts = e_tr(ps_bt.ap[:, j * P:(j + 1) * P],
                            srcb.ap[:, kc * P:(kc + 1) * P], idb.ap,
                            deps=srcb.wdeps() + ps_bt.wdeps() + idb.wdeps(),
                            inc="pe")
                srcb.used(evts)
                idb.used(evts)
            ps_bt.wrote(evts)
            stg = stages[qtr]
            dst = stg.ap.rearrange("p (kc t) -> p kc t",
                                   kc=4)[:, :, s * P:(s + 1) * P]
            eve = e_copy("dve", dst,
                         ps_bt.ap[:, 0:512].rearrange("p (j t) -> p j t",
                                                      j=4),
                         deps=[evts] + stg.wdeps(), inc="dve")
            ps_bt.used(eve)
            stg.wrote(eve)

    for b in range(BC):
        for tt in range(NTT):
            t0 = tt * TT
            for s in range(NSUB):
                si = s % 2
                xb = xsub[si]
                ev_x = e_dma("sp", xb.ap, x_in[b, t0 + s * P:t0 + s * P + P,
                                               :], deps=xb.wdeps())
                xb.wrote(ev_x)
                layer_norm_tm(xb, ln0b, 0)
                layer_norm_tm(ln0b, t1b, 5)
                ev_xn = e_tt("dve", xnb.ap, t1b.ap, wbB.ap, ALU.mult,
                             deps=t1b.wdeps() + wbB.wdeps() + xnb.wdeps(),
                             inc="dve")
                t1b.used(ev_xn)
                wbB.used(ev_xn)
                ev_xn = e_tt("dve", xnb.ap, xnb.ap, bbB.ap, ALU.add,
                             deps=[ev_xn] + bbB.wdeps(), inc="dve")
                bbB.used(ev_xn)
                xnb.wrote(ev_xn)
                stage_transpose(xnb, xn_stage, s)
                stage_transpose(ln0b, ln_stage, s)
            ev_mm = [None, None]
            for mc in range(2):
                psd = (ps_a0, ps_a1)[mc]
                for kc in range(DCN):
                    stg = xn_stage[kc // 4]
                    rhs = stg.ap[:, (kc % 4) * TT:(kc % 4 + 1) * TT]
                    lhsT = xpT.ap[:, (kc * 2 + mc) * P:(kc * 2 + mc + 1) * P]
                    e = e_mm(psd.ap, lhsT, rhs, kc == 0, kc == DCN - 1,
                             deps=stg.wdeps() + xpT.wdeps() + psd.wdeps(),
                             inc="pe")
                    stg.used(e)
                    xpT.used(e)
                    ev_mm[mc] = e
                psd.wrote(ev_mm[mc])
            ev_dr = e_act(dr_sb.ap[:, b * T + t0:b * T + t0 + TT], ps_a0.ap,
                          AF.Identity, deps=ps_a0.wdeps() + dr_sb.wdeps(),
                          inc="act")
            ps_a0.used(ev_dr)
            dr_sb.wrote(ev_dr)
            ev_bc = e_copy("dve", bc_stage.ap[:, 0:TT], ps_a1.ap,
                           deps=ps_a1.wdeps() + bc_stage.wdeps(), inc="dve")
            ps_a1.used(ev_bc)
            bc_stage.wrote(ev_bc)
            ev = e_dma("sp", bc_dram[b, :, t0:t0 + TT], bc_stage.ap[:, 0:TT],
                       deps=bc_stage.wdeps())
            bc_stage.used(ev)
            ev_bc_store[b] = ev
            for qtr in range(4):
                for stages, dram, evd in ((xn_stage, xn_dram, ev_xn_store),
                                          (ln_stage, ln_dram, ev_ln_store)):
                    stg = stages[qtr]
                    ev = e_dma(
                        "sp",
                        dram[b].rearrange("(kc p) t -> p kc t", p=P)
                        [:, qtr * 4:(qtr + 1) * 4, t0:t0 + TT],
                        stg.ap.rearrange("p (kc t) -> p kc t", kc=4),
                        deps=stg.wdeps())
                    stg.used(ev)
                    evd[b] = ev

    Cb0.ready = xnb.ready
    Cb0.consumers = list(xnb.consumers)
    pl.mark("phA_end")
    abuf = (a0, a1)
    qbuf = (q0, q1)
    hpair = (h0p, h1p)
    wpair = (w0p, w1p)
    Bpair = (Bb0, Bb1)
    Cpair = (Cb0, Cb1)
    dbuf = (d0, d1)
    ubuf = (u0, u1)
    xnsb = (xns, xns)
    lnsb = (lns, lns)

    ev_z_store = {}
    free_tree = list(tree)
    ev_scanN = {}
    ev_wN = {}
    ev_qN = {}

    prep_state = {}

    def emit_S_prep(b, dc):
        if True:
            di = dc % 2
            lhsT = dtT.ap[:, dc * P:(dc + 1) * P]
            ev_dt = None
            for j in range(4):
                rhs = dr_sb.ap[:, b * T + j * 512:b * T + (j + 1) * 512]
                ev_dt = e_mm(ps_t.ap[:, j * 512:(j + 1) * 512], lhsT, rhs,
                             True, True,
                             deps=dr_sb.wdeps() + dtT.wdeps() + ps_t.wdeps(),
                             inc="pe")
                dr_sb.used(ev_dt)
                dtT.used(ev_dt)
            ps_t.wrote(ev_dt)
            dcol = cst.ap[:, dc:dc + 1]
            dl = dbuf[di]
            ev1 = e_act(dl.ap, ps_t.ap, AF.Abs, bias=dcol,
                        deps=ps_t.wdeps() + dl.wdeps() + cst.wdeps(),
                        inc="act")
            ps_t.used(ev1)
            cst.used(ev1)
            ev2 = e_act(dl.ap, dl.ap, AF.Exp, scale=-1.0, deps=[ev1],
                        inc="act")
            ev3 = e_act(dl.ap, dl.ap, AF.Ln, bias=1.0, deps=[ev2], inc="act")
            ev4 = e_act(spbuf.ap, ps_t.ap, AF.Relu, bias=dcol,
                        deps=ps_t.wdeps() + spbuf.wdeps(), inc="act")
            ps_t.used(ev4)
            spbuf.wrote(ev4)
            ev_d = e_tt("dve", dl.ap, dl.ap, spbuf.ap, ALU.add,
                        deps=[ev3, ev4], inc="dve")
            spbuf.used(ev_d)
            dl.wrote(ev_d)
            xl = xnsb[di]
            ev_xl = e_dma("sp", xl.ap, xn_dram[b, dc * P:(dc + 1) * P, :],
                          deps=xl.wdeps() + [ev_xn_store.get(b)])
            xl.wrote(ev_xl)
            ub = ubuf[di]
            ev_u = e_tt("dve", ub.ap, dl.ap, xl.ap, ALU.mult,
                        deps=dl.wdeps() + xl.wdeps() + ub.wdeps(),
                        inc="dve")
            dl.used(ev_u)
            xl.used(ev_u)
            ub.wrote(ev_u)
            prep_state[(b, dc)] = (dl, ub)

    def emit_S_pairs(b, dc, hook=None):
        if True:
            di = dc % 2
            dl, ub = prep_state[(b, dc)]
            lnl = lnsb[di]
            ev_ld = e_dma("sp", lnl.ap, ln_dram[b, dc * P:(dc + 1) * P, :],
                          deps=lnl.wdeps() + [ev_ln_store.get(b)])
            lnl.wrote(ev_ld)
            stack = []

            def emit_w(k):
                hb = hpair[k % 2]
                Cb = Cpair[k % 2]
                wtgt = wpair[k % 2]
                ev_w = e_tt("dve", wtgt.ap, hb.ap, Cb.ap, ALU.mult,
                            deps=hb.wdeps() + Cb.wdeps() + wtgt.wdeps(),
                            inc="dve")
                hb.used(ev_w)
                Cb.used(ev_w)
                wtgt.wrote(ev_w)
                ev_wN[k] = ev_w

            def emit_pair(k):
                wp = wpair[k % 2]
                if k < NST // 2 - 1:
                    tb = free_tree.pop()
                    ev_sg = e_tt("dve", tb.ap, wp.ap[:, 0:T],
                                 wp.ap[:, T:2 * T], ALU.add,
                                 deps=wp.wdeps() + tb.wdeps(), inc="dve")
                    wp.used(ev_sg)
                    tb.wrote(ev_sg)
                    stack.append([0, tb])
                    while len(stack) >= 2 and stack[-1][0] == stack[-2][0]:
                        l2, b2 = stack.pop()
                        l1, b1 = stack.pop()
                        evm = e_tt("dve", b1.ap, b1.ap, b2.ap, ALU.add,
                                   deps=b1.wdeps() + b2.wdeps(), inc="dve")
                        b1.wrote(evm)
                        b2.used(evm)
                        free_tree.append(b2)
                        stack.append([l1 + 1, b1])
                else:
                    l1, b1 = stack[-1]
                    evm = e_tt("dve", b1.ap, b1.ap, wp.ap[:, 0:T], ALU.add,
                               deps=b1.wdeps() + wp.wdeps(), inc="dve")
                    b1.wrote(evm)
                    evm = e_tt("dve", b1.ap, b1.ap, wp.ap[:, T:2 * T],
                               ALU.add, deps=b1.wdeps() + wp.wdeps(),
                               inc="dve")
                    wp.used(evm)
                    b1.wrote(evm)

            for k in range(NST // 2):
                ns = (2 * k, 2 * k + 1)
                Bb = Bpair[k % 2]
                evB = e_dma("sp", Bb.ap,
                            bc_dram[b, 2 * k:2 * k + 2, :]
                            .rearrange("a t -> (a t)")[None, :]
                            .partition_broadcast(P),
                            deps=Bb.wdeps() + [ev_bc_store.get(b)])
                Bb.wrote(evB)
                Cb = Cpair[k % 2]
                evC = e_dma("sp", Cb.ap,
                            bc_dram[b, NST + 2 * k:NST + 2 * k + 2, :]
                            .rearrange("a t -> (a t)")[None, :]
                            .partition_broadcast(P),
                            deps=Cb.wdeps() + [ev_bc_store.get(b)])
                Cb.wrote(evC)
                if hook is not None and k == 3:
                    hook()
                for n in ns:
                    ab = abuf[n % 2]
                    scl = A_sb.ap[:, dc * NST + n:dc * NST + n + 1]
                    ev_a = e_act(ab.ap, dl.ap, AF.Exp, scale=scl,
                                 deps=dl.wdeps() + ab.wdeps() +
                                 A_sb.wdeps(), inc="act")
                    dl.used(ev_a)
                    A_sb.used(ev_a)
                    ab.wrote(ev_a)
                for n in ns:
                    qb = qbuf[n % 2]
                    ev_q = e_tt("dve", qb.ap, ub.ap,
                                Bb.ap[:, (n % 2) * T:(n % 2 + 1) * T],
                                ALU.mult,
                                deps=ub.wdeps() + Bb.wdeps() + qb.wdeps(),
                                inc="dve")
                    ub.used(ev_q)
                    Bb.used(ev_q)
                    qb.wrote(ev_q)
                if k >= 1:
                    emit_w(k - 1)
                if k >= 2:
                    emit_pair(k - 2)
                for n in ns:
                    ab = abuf[n % 2]
                    qb = qbuf[n % 2]
                    hb = hpair[k % 2]
                    ev_sc = e_scan(hb.ap[:, (n % 2) * T:(n % 2 + 1) * T],
                                   ab.ap, qb.ap,
                                   deps=ab.wdeps() + qb.wdeps() +
                                   hb.wdeps(), inc="dve")
                    ab.used(ev_sc)
                    qb.used(ev_sc)
                    hb.wrote(ev_sc)
                    ev_scanN[n] = ev_sc
            emit_w(NST // 2 - 1)
            emit_pair(NST // 2 - 2)
            emit_pair(NST // 2 - 1)
            while len(stack) > 1:
                l2, b2 = stack.pop()
                l1, b1 = stack.pop()
                evm = e_tt("dve", b1.ap, b1.ap, b2.ap, ALU.add,
                           deps=b1.wdeps() + b2.wdeps(), inc="dve")
                b1.wrote(evm)
                b2.used(evm)
                free_tree.append(b2)
                stack.append([l1 + 1, b1])
            _, ybf = stack.pop()
            lnl = lnsb[di]
            ztb_ = free_tree[0]
            ev_z = e_stt("dve", ztb_.ap, lnl.ap, cst.ap[:, 2 * DCN + dc:
                                                        2 * DCN + dc + 1],
                         ybf.ap, ALU.mult, ALU.add,
                         deps=lnl.wdeps() + ybf.wdeps() + ztb_.wdeps() +
                         cst.wdeps(), inc="dve")
            lnl.used(ev_z)
            ybf.used(ev_z)
            cst.used(ev_z)
            ztb_.wrote(ev_z)
            free_tree.append(ybf)
            ev_zs = e_dma("sp", z_dram[b, dc * P:(dc + 1) * P, :], ztb_.ap,
                          deps=ztb_.wdeps())
            ztb_.used(ev_zs)
            ev_z_store[b] = ev_zs

    pl.mark("phS_end")
    ztb = (q0, q1, Bb2, Bb3)
    lnstage = (Buf(Bb0.ap[:, 0:T]), Buf(Bb0.ap[:, T:2 * T]),
               Buf(Bb1.ap[:, 0:T]), Buf(Bb1.ap[:, T:2 * T]))
    lnspar = {0: Bb0, 1: Bb0, 2: Bb1, 3: Bb1}
    afb = (wb0, wb1, tree[0], tree[1])
    lwb = (tree[2], tree[3])
    x1b = spbuf
    xrl = (d0, d1)
    sqp = u0
    ln1b = u1
    midb = [Buf(w0p.ap[0:LR, i * TT:(i + 1) * TT]) for i in range(2)]
    midgb = [Buf(w0p.ap[0:LR, (2 + i) * TT:(3 + i) * TT]) for i in range(2)]
    gbb = [Buf(w0p.ap[0:LR, (4 + i) * TT:(5 + i) * TT]) for i in range(2)]
    gateb = Buf(w0p.ap[:, 6 * TT:6 * TT + 32])
    gTs = Buf(w0p.ap[0:32, 6 * TT + 32:6 * TT + 32 + P])
    mpack = Buf(w1p.ap[:, 0:TT])
    lgb = Buf(a1.ap[0:NEXP, 0:TT])
    subs = {"w0p": (w0p, midb + midgb + gbb + [gateb, gTs]),
            "w1p": (w1p, [mpack]),
            "a1": (a1, [lgb])}

    def parent_sync(name):
        par, kids = subs[name]
        for k in kids:
            k.ready = par.ready
            k.consumers = list(par.consumers)

    for nm in subs:
        parent_sync(nm)

    def emit_P_tt(b, tt):
        if True:
            for qi in range(4):
                kid = lnstage[qi]
                par = lnspar[qi]
                kid.ready = par.ready
                kid.consumers = list(par.consumers)
            t0 = tt * TT
            tdi = b * NTT + tt
            for qtr in range(4):
                zt = ztb[qtr]
                ev = e_dma(
                    "sp", zt.ap.rearrange("p (kc t) -> p kc t", kc=4),
                    z_dram[b].rearrange("(kc p) t -> p kc t", p=P)
                    [:, qtr * 4:(qtr + 1) * 4, t0:t0 + TT],
                    deps=zt.wdeps() + [ev_z_store.get(b)])
                zt.wrote(ev)
            for mc in range(DCN):
                lw = lwb[mc % 2]
                ev_lw = e_dma("sp",
                              lw.ap.rearrange("p (kc m) -> p kc m", kc=DCN),
                              opT_dram[mc].rearrange("kc p m -> p kc m"),
                              deps=lw.wdeps())
                lw.wrote(ev_lw)
                psd = (ps_a0, ps_a1)[mc % 2]
                evm = None
                for kc in range(DCN):
                    zt = ztb[kc // 4]
                    rhs = zt.ap[:, (kc % 4) * TT:(kc % 4 + 1) * TT]
                    evm = e_mm(psd.ap, lw.ap[:, kc * P:(kc + 1) * P], rhs,
                               kc == 0, kc == DCN - 1,
                               deps=zt.wdeps() + lw.wdeps() + psd.wdeps(),
                               inc="pe")
                    zt.used(evm)
                    lw.used(evm)
                psd.wrote(evm)
                af = afb[mc // 4]
                ev = e_act(af.ap[:, (mc % 4) * TT:(mc % 4 + 1) * TT], psd.ap,
                           AF.Identity, deps=psd.wdeps() + af.wdeps(), inc="act")
                psd.used(ev)
                af.wrote(ev)
            for s in range(NSUB):
                si = s % 2
                xr = xrl[si]
                ev_x = e_dma("sp", xr.ap,
                             x_in[b, t0 + s * P:t0 + s * P + P, :],
                             deps=xr.wdeps())
                xr.wrote(ev_x)
                evt = None
                for mc in range(DCN):
                    af = afb[mc // 4]
                    src = af.ap[:, (mc % 4) * TT + s * P:
                                (mc % 4) * TT + (s + 1) * P]
                    evt = e_tr(ps_bt.ap[:, mc * P:(mc + 1) * P], src, idb.ap,
                               deps=af.wdeps() + ps_bt.wdeps() + idb.wdeps(),
                               inc="pe")
                    af.used(evt)
                    idb.used(evt)
                ps_bt.wrote(evt)
                ev_x1 = e_tt("dve", x1b.ap, xr.ap, ps_bt.ap, ALU.add,
                             deps=xr.wdeps() + ps_bt.wdeps() + x1b.wdeps(),
                             inc="dve")
                xr.used(ev_x1)
                ps_bt.used(ev_x1)
                x1b.wrote(ev_x1)
                ev_sp = e_dma("sp", x1_dram[b, t0 + s * P:t0 + s * P + P, :],
                              x1b.ap, deps=x1b.wdeps())
                x1b.used(ev_sp)
                layer_norm_tm(x1b, ln1b, 0)
                stage_transpose(ln1b, lnstage, s)
            evm = None
            for kc in range(DCN):
                stg = lnstage[kc // 4]
                rhs = stg.ap[:, (kc % 4) * TT:(kc % 4 + 1) * TT]
                evm = e_mm(ps_a0.ap[0:NEXP, :],
                           rT.ap[:, kc * NEXP:(kc + 1) * NEXP], rhs,
                           kc == 0, kc == DCN - 1,
                           deps=stg.wdeps() + rT.wdeps() + ps_a0.wdeps(),
                           inc="pe")
                stg.used(evm)
                rT.used(evm)
            ps_a0.wrote(evm)
            parent_sync("a1")
            ev_lg = e_act(lgb.ap, ps_a0.ap[0:NEXP, :], AF.Identity, bias=rb_c.ap,
                          deps=ps_a0.wdeps() + lgb.wdeps() + rb_c.wdeps(),
                          inc="act")
            ps_a0.used(ev_lg)
            rb_c.used(ev_lg)
            lgb.wrote(ev_lg)
            a1.wrote(ev_lg)
            for s in range(NSUB):
                evt = e_tr(ps_t.ap[0:P, s * NEXP:(s + 1) * NEXP],
                           lgb.ap[0:NEXP, s * P:(s + 1) * P],
                           idf.ap[0:NEXP, 0:NEXP],
                           deps=lgb.wdeps() + ps_t.wdeps() + idf.wdeps(),
                           inc="pe")
                lgb.used(evt)
                a1.used(evt)
                idf.used(evt)
                ps_t.wrote(evt)
                eve = e_copy("dve",
                             gsc.ap[:, 32 + s * NEXP:32 + (s + 1) * NEXP],
                             ps_t.ap[0:P, s * NEXP:(s + 1) * NEXP],
                             deps=[evt] + gsc.wdeps(), inc="dve")
                ps_t.used(eve)
                gsc.wrote(eve)
            for s in range(NSUB):
                lg = gsc.ap[:, 32 + s * NEXP:32 + (s + 1) * NEXP]
                m1 = gsc.ap[:, 64:65]
                nm1 = gsc.ap[:, 65 + s:66 + s]
                m2 = gsc.ap[:, 69:70]
                ddc = gsc.ap[:, 70 + s:71 + s]
                dnc = gsc.ap[:, 74:75]
                ivc = gsc.ap[:, 75:76]
                mask1 = gsc.ap[:, 80:88]
                lg2 = gsc.ap[:, 96:104]
                mask2 = gsc.ap[:, 104:112]
                ev1 = e_red(m1, lg, ALU.max, deps=gsc.wdeps() + stat.wdeps(),
                            inc="dve")
                evn = e_ts("dve", nm1, m1, -1.0, ALU.mult, deps=[ev1],
                           inc="dve", strict=True)
                ev3 = e_ts("dve", mask1, lg, m1, ALU.is_equal, deps=[ev1],
                           inc="dve", strict=True)
                ev4 = e_stt("dve", lg2, mask1, -1e30, lg, ALU.mult, ALU.add,
                            deps=[ev3], inc="dve", strict=True)
                ev5 = e_red(m2, lg2, ALU.max, deps=[ev4], inc="dve",
                            strict=True)
                ev6 = e_ts("dve", mask2, lg2, m2, ALU.is_equal, deps=[ev5],
                           inc="dve", strict=True)
                ev7 = e_act(lg, lg, AF.Exp, bias=nm1, deps=[ev5, evn],
                            inc="act", strict=True)
                ev8 = e_tt("dve", mask1, mask1, mask2, ALU.add,
                           deps=[ev6, ev7], inc="dve", strict=True)
                ev9 = e_tt("dve", ddc, m2, m1, ALU.subtract, deps=[ev5],
                           inc="dve", strict=True)
                ev10 = e_act(dnc, ddc, AF.Exp, deps=[ev9], inc="act",
                             strict=True)
                ev11 = e_ts("dve", dnc, dnc, 1.0, ALU.add, deps=[ev10],
                            inc="dve", strict=True)
                ev12 = e_recip(ivc, dnc, deps=[ev11], inc="dve", strict=True)
                ev13 = e_tt("dve", lg, lg, mask1, ALU.mult, deps=[ev7, ev8],
                            inc="dve", strict=True)
                gcols = gateb.ap.rearrange(
                    "p (e s2) -> p s2 e", s2=4)[:, s, :]
                ev14 = e_ts("dve", gcols, lg,
                            ivc, ALU.mult,
                            deps=[ev12, ev13] + gateb.wdeps(), inc="dve",
                            strict=True)
                gateb.wrote(ev14)
                w0p.wrote(ev14)
                gsc.wrote(ev14)
            evt = e_tr(ps_bt.ap[0:32, 0:P], gateb.ap, idb.ap,
                       deps=gateb.wdeps() + ps_bt.wdeps() + idb.wdeps(),
                       inc="pe")
            gateb.used(evt)
            w0p.used(evt)
            idb.used(evt)
            ps_bt.wrote(evt)
            eve = e_copy("dve", gTs.ap, ps_bt.ap[0:32, 0:P],
                         deps=[evt] + gTs.wdeps(), inc="dve")
            ps_bt.used(eve)
            gTs.wrote(eve)
            w0p.wrote(eve)
            ev_gst = e_dma("sp", g_dram[tdi], gTs.ap,
                           deps=gTs.wdeps())
            gTs.used(ev_gst)
            w0p.used(ev_gst)
            parent_sync("w0p")
            parent_sync("w1p")
            ev_pack = None
            for e_ in range(NEXP):
                ei = e_ % 2
                evm = None
                for kc in range(DCN):
                    stg = lnstage[kc // 4]
                    rhs = stg.ap[:, (kc % 4) * TT:(kc % 4 + 1) * TT]
                    lhsT = laT.ap[:, (kc * NEXP + e_) * LR:
                                  (kc * NEXP + e_ + 1) * LR]
                    evm = e_mm(ps_a0.ap[0:LR, :], lhsT, rhs, kc == 0,
                               kc == DCN - 1,
                               deps=stg.wdeps() + laT.wdeps() +
                               ps_a0.wdeps(), inc="pe")
                    stg.used(evm)
                    laT.used(evm)
                ps_a0.wrote(evm)
                md = midb[ei]
                ev_md = e_act(md.ap, ps_a0.ap[0:LR, :], AF.Identity,
                              deps=ps_a0.wdeps() + md.wdeps(), inc="act")
                ps_a0.used(ev_md)
                md.wrote(ev_md)
                w0p.used(ev_md)
                gb = gbb[ei]
                gsrc = g_dram[tdi].rearrange(
                    "(e s2) j -> e (s2 j)", s2=4)[e_:e_ + 1, :]
                ev_gb = e_dma("sp", gb.ap,
                              gsrc.partition_broadcast(LR),
                              deps=gb.wdeps() + [ev_gst])
                gb.wrote(ev_gb)
                w0p.used(ev_gb)
                mg = midgb[ei]
                ev_mg = e_tt("dve", mg.ap, md.ap, gb.ap, ALU.mult,
                             deps=md.wdeps() + gb.wdeps() + mg.wdeps(),
                             inc="dve")
                md.used(ev_mg)
                gb.used(ev_mg)
                mg.wrote(ev_mg)
                w0p.used(ev_mg)
                ev_pack = e_mm(ps_a1.ap, selb.ap[:, e_ * P:(e_ + 1) * P],
                               mg.ap, e_ == 0, e_ == NEXP - 1,
                               deps=mg.wdeps() + selb.wdeps() +
                               ps_a1.wdeps(), inc="pe")
                mg.used(ev_pack)
                selb.used(ev_pack)
            ps_a1.wrote(ev_pack)
            ev_mp = e_act(mpack.ap, ps_a1.ap, AF.Identity,
                          deps=ps_a1.wdeps() + mpack.wdeps(), inc="act")
            ps_a1.used(ev_mp)
            mpack.wrote(ev_mp)
            w1p.wrote(ev_mp)
            for mc in range(DCN):
                lw = lwb[mc % 2]
                ev_lw = e_dma("sp",
                              lw.ap.rearrange("p (kc m) -> p kc m", kc=DCN),
                              baT_dram[mc].rearrange("kc p m -> p kc m"),
                              deps=lw.wdeps())
                lw.wrote(ev_lw)
                psd = (ps_a0, ps_a1)[mc % 2]
                evm = e_mm(psd.ap, lbT.ap[:, mc * P:(mc + 1) * P], mpack.ap,
                           True, False,
                           deps=mpack.wdeps() + lbT.wdeps() + psd.wdeps(),
                           inc="pe")
                mpack.used(evm)
                w1p.used(evm)
                lbT.used(evm)
                for kc in range(DCN):
                    stg = lnstage[kc // 4]
                    rhs = stg.ap[:, (kc % 4) * TT:(kc % 4 + 1) * TT]
                    evm = e_mm(psd.ap, lw.ap[:, kc * P:(kc + 1) * P], rhs,
                               False, kc == DCN - 1,
                               deps=stg.wdeps() + lw.wdeps(),
                               inc="pe")
                    stg.used(evm)
                    lw.used(evm)
                psd.wrote(evm)
                af = afb[mc // 4]
                ev = e_act(af.ap[:, (mc % 4) * TT:(mc % 4 + 1) * TT], psd.ap,
                           AF.Identity, bias=cst.ap[:, DCN + mc:DCN + mc + 1],
                           deps=psd.wdeps() + af.wdeps() + cst.wdeps(),
                           inc="act")
                psd.used(ev)
                cst.used(ev)
                af.wrote(ev)
            for s in range(NSUB):
                si = s % 2
                xr = xrl[si]
                ev_x1r = e_dma("sp", xr.ap,
                               x1_dram[b, t0 + s * P:t0 + s * P + P, :],
                               deps=xr.wdeps())
                xr.wrote(ev_x1r)
                evt = None
                for mc in range(DCN):
                    af = afb[mc // 4]
                    src = af.ap[:, (mc % 4) * TT + s * P:
                                (mc % 4) * TT + (s + 1) * P]
                    evt = e_tr(ps_bt.ap[:, mc * P:(mc + 1) * P], src, idb.ap,
                               deps=af.wdeps() + ps_bt.wdeps() + idb.wdeps(),
                               inc="pe")
                    af.used(evt)
                    idb.used(evt)
                ps_bt.wrote(evt)
                ev_o = e_tt("dve", xr.ap, xr.ap, ps_bt.ap, ALU.add,
                            deps=xr.wdeps() + ps_bt.wdeps(), inc="dve")
                ps_bt.used(ev_o)
                xr.wrote(ev_o)
                ev_os = e_dma("sp", out_d[b, t0 + s * P:t0 + s * P + P, :],
                              xr.ap, deps=xr.wdeps(), inc="st")
                xr.used(ev_os)
            for qi in range(4):
                kid = lnstage[qi]
                par = lnspar[qi]
                par.used(kid.ready)
                for c in kid.consumers:
                    par.used(c)

    emit_S_prep(0, 0)
    for dc in range(DCN):
        def hook(dc=dc):
            if dc + 1 < DCN:
                emit_S_prep(0, dc + 1)
        emit_S_pairs(0, dc, hook)
    pl.mark("b0S_end")
    for tt in range(NTT):
        emit_P_tt(0, tt)
    emit_S_prep(1, 0)
    pl.mark("b0P_end")
    for dc in range(DCN):
        def hook(dc=dc):
            if dc + 1 < DCN:
                emit_S_prep(1, dc + 1)
        emit_S_pairs(1, dc, hook)
    pl.mark("b1S_end")
    for tt in range(NTT):
        emit_P_tt(1, tt)
    pl.mark("b1P_end")

    n_out = pl.count["st"]

    def fin(e):
        return e.nop()
    pl.emit("sp", fin, deps=[("st", n_out)], inc=None)

    with (
        nc.semaphore("s_sp") as s_sp,
        nc.semaphore("s_gp") as s_gp,
        nc.semaphore("s_st") as s_st,
        nc.semaphore("s_act") as s_act,
        nc.semaphore("s_dve") as s_dve,
        nc.semaphore("s_pool") as s_pool,
        nc.semaphore("s_pe") as s_pe,
        nc.semaphore("s_rdy") as s_rdy,
        nc.semaphore("s_rdyg") as s_rdyg,
        nc.Block() as block,
    ):
        sems = dict(sp=s_sp, gp=s_gp, st=s_st, act=s_act, dve=s_dve,
                    pool=s_pool, pe=s_pe, rdy=s_rdy, rdyg=s_rdyg)

        @block.sync
        def _(eng):
            pl.replay("sp", eng, sems, nc)

        @block.scalar
        def _(eng):
            pl.replay("act", eng, sems, nc)

        @block.vector
        def _(eng):
            pl.replay("dve", eng, sems, nc)

        @block.gpsimd
        def _(eng):
            pl.replay("gp", eng, sems, nc)
            pl.replay("pool", eng, sems, nc)

        @block.tensor
        def _(eng):
            pl.replay("pe", eng, sems, nc)

    for cm in reversed(ctxs):
        cm.__exit__(None, None, None)

    return nc


_NC_CACHE = None


def _consts():
    idb = np.eye(P, dtype=ml_dtypes.bfloat16)
    idf = np.eye(P, dtype=np.float32)
    oneb = np.ones((1, P), dtype=ml_dtypes.bfloat16)
    selb = np.zeros((LR, NEXP * P), dtype=ml_dtypes.bfloat16)
    for e in range(NEXP):
        for k in range(LR):
            selb[k, e * P + LR * e + k] = 1.0
    return idb, idf, oneb, selb


def kernel(L0_attn_x, L0_attn_norm_w, L0_attn_norm_b, L0_attn_x_proj_w,
           L0_attn_dt_proj_w, L0_attn_dt_proj_b, L0_attn_A_log,
           L0_attn_D_param, L0_attn_out_proj_w, L0_ffn_base_w, L0_ffn_base_b,
           L0_ffn_router_w, L0_ffn_router_b, L0_ffn_lora_A, L0_ffn_lora_B):
    global _NC_CACHE
    if _NC_CACHE is None:
        _NC_CACHE = build_program()
    nc = _NC_CACHE

    idb, idf, oneb, selb = _consts()
    x = np.ascontiguousarray(L0_attn_x)
    common = dict(
        normw=L0_attn_norm_w, normb=L0_attn_norm_b, xproj=L0_attn_x_proj_w,
        dtw=L0_attn_dt_proj_w, dtb=L0_attn_dt_proj_b, alog=L0_attn_A_log,
        dpar=L0_attn_D_param, opw=L0_attn_out_proj_w, basew=L0_ffn_base_w,
        baseb=L0_ffn_base_b, routw=L0_ffn_router_w, routb=L0_ffn_router_b,
        loraA=L0_ffn_lora_A, loraB=L0_ffn_lora_B,
        idb=idb, idf=idf, oneb=oneb, selb=selb,
    )
    common = {k: np.ascontiguousarray(v) for k, v in common.items()}
    in_maps = []
    for c in range(8):
        m = dict(common)
        m["x"] = x[c * BC:(c + 1) * BC]
        in_maps.append(m)
    res = run_bass_kernel_spmd(nc, in_maps, core_ids=list(range(8)))
    return np.concatenate([r["out"] for r in res.results], axis=0)


if __name__ == "__main__":
    nc = build_program()
    n = sum(1 for _ in nc.all_instructions())
    print("built ok, instructions:", n)
